# version 07
# baseline (speedup 1.0000x reference)
import numpy as np
import concourse.bass as bass
import concourse.mybir as mybir
from concourse.bass_utils import run_bass_kernel_spmd

F32 = mybir.dt.float32
BF16 = mybir.dt.bfloat16
U8 = mybir.dt.uint8
AF = mybir.ActivationFunctionType
ALU = mybir.AluOpType
AX = mybir.AxisListType

B, S, D = 16, 2048, 1024
H, FD, HD = 16, 16, 64
NE, DI = 16, 512
NT = S // 128
NCH = 4
SEQN = 2
EPS = 1e-5

class Tok:
    __slots__ = ("sem", "val")
    def __init__(self, sem, val):
        self.sem, self.val = sem, val

class SemC:
    def __init__(self, nc, name):
        self.h = nc.alloc_semaphore(name=name)
        self.n = 0
    def tok(self):
        return Tok(self, self.n)

class EngQ:
    def __init__(self, name):
        self.name = name
        self.ops = []
        self.mark = {}
    def wait(self, tok):
        if tok is None:
            return
        k = id(tok.sem)
        if self.mark.get(k, 0) >= tok.val:
            return
        self.mark[k] = tok.val
        sem_h, v = tok.sem.h, tok.val
        self.ops.append(lambda h: h.wait_ge(sem_h, v))
    def waits(self, toks):
        for t in toks:
            self.wait(t)
    def do(self, fn, inc=None, n=1):
        if inc is not None:
            inc.n += n
            sem_h, nn = inc.h, n
            def f(h, fn=fn):
                fn(h).then_inc(sem_h, nn)
            self.ops.append(f)
            return inc.tok()
        self.ops.append(lambda h, fn=fn: fn(h))
        return None

class Bank:
    def __init__(self, ap):
        self.ap = ap
        self.wtok = None
        self.rtoks = []
    def pre_write(self, eng):
        eng.waits(self.rtoks)
    def set_written(self, tok):
        self.wtok = tok
        self.rtoks = []
    def pre_read(self, eng):
        eng.wait(self.wtok)
    def add_reader(self, tok):
        if tok is not None:
            self.rtoks.append(tok)


def build_program(debug=0, stage=4):
    nc = bass.Bass("TRN2", target_bir_lowering=False, detect_race_conditions=False)

    x_in = nc.dram_tensor("x", [SEQN, S, D], F32, kind="ExternalInput").ap()
    wqkvT_d = nc.dram_tensor("wqkvT", [D, 1536], F32, kind="ExternalInput").ap()
    woT_d = nc.dram_tensor("woT", [D, D], F32, kind="ExternalInput").ap()
    gu_d = nc.dram_tensor("gu", [NE + 1, D, 2 * DI], F32, kind="ExternalInput").ap()
    dn_d = nc.dram_tensor("dn", [NE + 1, DI, D], F32, kind="ExternalInput").ap()
    gwT_d = nc.dram_tensor("gwT_in", [D, NE], F32, kind="ExternalInput").ap()
    gam_d = nc.dram_tensor("gam", [1, FD], F32, kind="ExternalInput").ap()
    bet_d = nc.dram_tensor("bet", [1, FD], F32, kind="ExternalInput").ap()
    ebias_d = nc.dram_tensor("ebias", [1, NE], F32, kind="ExternalInput").ap()
    y_out = nc.dram_tensor("y", [SEQN, S, D], F32, kind="ExternalOutput").ap()
    rz_dr = nc.dram_tensor("rz_scr", [2, 16, 512], BF16, kind="Internal").ap()

    dbg = {}
    def dbgt(name, shape):
        dbg[name] = nc.dram_tensor("d_" + name, shape, F32, kind="ExternalOutput").ap()
    if debug:
        dbgt("x1", [128, D]); dbgt("lns", [128, 8]); dbgt("x1t", [128, 8 * 128])
        dbgt("qT", [128, 4 * S]); dbgt("kT", [128, 4 * S]); dbgt("v", [128, NT * H * 65])
        dbgt("onT", [128, 8 * S]); dbgt("x2", [128, NT * D]); dbgt("W", [16, S])
        dbgt("x2t", [128, 8 * S])

    def sbuf(name, shape, dtype):
        return nc.alloc_sbuf_tensor(name, shape, dtype)

    xa = sbuf("xa", [128, 2, D], F32)
    xrq = sbuf("xrq", [128, 2, D], F32)
    sq_sc = sbuf("sq_sc", [128, D], F32)
    x1t = sbuf("x1t", [128, 8, 128], BF16)
    x2t = sbuf("x2t", [128, 8, S], BF16)
    st2 = sbuf("st2", [128, 2, 4, 512], BF16)
    gwT = sbuf("gwT", [128, 8, 16], BF16)
    wt = sbuf("wt", [16, S], BF16)
    zt = sbuf("zt", [16, 512], F32)
    rz = sbuf("rz", [16, 512], BF16)
    zbuf = sbuf("zbuf", [64, 4, 512], BF16)
    selm = sbuf("selm", [16, 1024], BF16)
    selk = sbuf("selk", [16, 1], F32)
    onesr = sbuf("onesr", [1, 128], F32)
    ident = sbuf("ident", [128, 128], F32)
    tril = sbuf("tril", [128, 128], BF16)
    gamq = sbuf("gamq", [128, 256], F32)
    gamk = sbuf("gamk", [128, 256], F32)
    betq = sbuf("betq", [128, 256], F32)
    betk = sbuf("betk", [128, 256], F32)
    ebias_t = sbuf("ebias_t", [128, 16], F32)
    ceps = sbuf("ceps", [128, 1], F32)
    gsmall = sbuf("gsmall", [1, FD], F32)
    bsmall = sbuf("bsmall", [1, FD], F32)
    ebrow = sbuf("ebrow", [1, NE], F32)
    lns = sbuf("lns", [128, 8], F32)
    fms = sbuf("fms", [128, 8, 16], F32)
    gts = sbuf("gts", [128, 80], F32)
    intT = sbuf("intT", [128, 2, 256], BF16)
    sgsc = sbuf("sgsc", [128, 256], F32)
    t1sc = sbuf("t1sc", [128, 256], F32)
    websb = sbuf("websb", [128, 256], BF16)
    junkv = sbuf("junkv", [128, 8], F32)

    qkpad = xrq[:].rearrange("p a d -> p (a d)")[:, 0:1024].rearrange("p (a b) -> p a b", a=2)
    fmsc = xrq[:].rearrange("p a d -> p (a d)")[:, 1024:1536].rearrange("p (a b) -> p a b", a=2)
    xr = xrq

    ar1 = sbuf("ar1", [128, 66048], U8)
    qT = ar1[:, 0:16384].bitcast(BF16).rearrange("p (c t) -> p c t", c=4)
    kT = ar1[:, 16384:32768].bitcast(BF16).rearrange("p (c t) -> p c t", c=4)
    vsb = ar1[:, 32768:66048].bitcast(BF16).rearrange("p (t h c) -> p t h c", t=NT, h=H)
    ysb = ar1[:, 0:65536].bitcast(F32).rearrange("p (t d) -> p t d", t=NT)

    ar2 = sbuf("ar2", [128, 32 * 1024], U8)
    x1f = ar2[:, 0:8192].bitcast(F32).rearrange("p (a d) -> p a d", a=2)
    onT = ar2[:, 0:32768].bitcast(BF16).rearrange("p (c t) -> p c t", c=8)
    gusb = [ar2[:, i * 16384:(i + 1) * 16384].bitcast(BF16).rearrange("p (c n) -> p c n", c=8)
            for i in range(2)]

    ar3 = sbuf("ar3", [128, 24 * 1024], U8)
    wqkv = ar3[:, 0:24576].bitcast(BF16).rearrange("p (c n) -> p c n", c=8)
    woT = ar3[:, 0:16384].bitcast(BF16).rearrange("p (c n) -> p c n", c=8)
    orw = ar3[0:65, 0:16384].bitcast(BF16).rearrange("p (h t) -> p h t", h=16)
    ostg = ar3[0:64, 16384:24576].bitcast(BF16).rearrange("p (h t) -> p h t", h=8)
    dnsb = [ar3[:, i * 8192:(i + 1) * 8192].bitcast(BF16)
            .rearrange("p (c n) -> p c n", c=4) for i in range(2)]

    PA = nc.alloc_psum_tensor("PA", [128, 4096], F32)
    banks = [Bank(PA[:, 512 * i:512 * (i + 1)]) for i in range(8)]

    s_xa = [SemC(nc, "sxa0"), SemC(nc, "sxa1")]
    s_sh = SemC(nc, "ssh"); s_st = SemC(nc, "sst")
    g_cst = SemC(nc, "gcst"); g_wq = SemC(nc, "gwq"); g_wo = SemC(nc, "gwo")
    g_buf = [SemC(nc, "gbuf0"), SemC(nc, "gbuf1")]
    g_dnb = [SemC(nc, "gdnb0"), SemC(nc, "gdnb1")]
    g_z = SemC(nc, "gz"); g_dbg = SemC(nc, "gdbg"); s_zb = SemC(nc, "szb")
    te = SemC(nc, "te"); ve = SemC(nc, "ve"); se = SemC(nc, "se"); gc = SemC(nc, "gc")
    DMAS = [s_xa[0], s_xa[1], s_sh, s_st, g_cst, g_wq, g_wo, g_buf[0], g_buf[1],
            g_dnb[0], g_dnb[1], g_z, g_dbg, s_zb]
    Y = EngQ("sync"); G = EngQ("gpsimd"); V = EngQ("vector"); C = EngQ("scalar"); T = EngQ("tensor")
    ALLQ = [Y, G, V, C, T]

    def vsep():
        V.do(lambda h: h.tensor_scalar_mul(junkv[:, 0:1], junkv[:, 1:2], 1.0))

    def dma(q, sem, out, in_):
        return q.do(lambda h, o=out, i=in_: h.dma_start(out=o, in_=i), inc=sem, n=16)

    def barrier():
        toks = [s.tok() for s in DMAS] + [te.tok(), ve.tok(), se.tok(), gc.tok()]
        for q in ALLQ:
            q.waits(toks)

    dma(G, g_cst, gsmall[:], gam_d)
    dma(G, g_cst, bsmall[:], bet_d)
    dma(G, g_cst, ebrow[:], ebias_d)
    t_gwT = dma(G, g_cst, gwT[:], gwT_d.rearrange("(c p) n -> p c n", p=128))
    t_gd1 = t_gwT

    iorow = sq_sc[:, 0:128]
    iocol = sq_sc[:, 128:256]
    G.do(lambda h: h.iota(iocol, [[1, 128]], channel_multiplier=0,
                          allow_small_or_imprecise_dtypes=True))
    G.do(lambda h: h.iota(iorow, [[0, 128]], channel_multiplier=1,
                          allow_small_or_imprecise_dtypes=True))
    G.do(lambda h: h.iota(selm[:], [[1, 16], [0, 64]], channel_multiplier=0,
                          allow_small_or_imprecise_dtypes=True))
    t_gc0 = G.do(lambda h: h.iota(selk[:], [[0, 1]], channel_multiplier=1,
                                  allow_small_or_imprecise_dtypes=True), inc=gc)
    V.wait(t_gc0)
    V.do(lambda h: h.tensor_tensor(ident[:], iocol, iorow, ALU.is_equal))
    V.do(lambda h: h.tensor_tensor(tril[:], iorow, iocol, ALU.is_le))
    V.do(lambda h: h.memset(ceps[:], EPS))
    V.do(lambda h: h.memset(lns[:], 0.0))
    V.do(lambda h: h.memset(junkv[:], 0.0))
    V.do(lambda h: h.memset(onesr[:], 1.0))
    V.do(lambda h: h.memset(
        qkpad.rearrange("p a (c g f) -> p a c g f", c=4, g=4)[:, :, :, :, 16:32], 0.0))
    t_sel = V.do(lambda h: h.tensor_scalar(selm[:], selm[:], selk[:, 0:1], None,
                                           ALU.is_equal), inc=ve)

    T.wait(t_gd1); T.wait(t_sel)
    b0, b1 = banks[0], banks[1]
    b0.pre_write(T)
    T.do(lambda h: h.matmul(b0.ap[0:128, 0:256], onesr[0:1, :],
                            gsmall[0:1, :].unsqueeze(1).broadcast_to([1, 16, FD]),
                            start=True, stop=False))
    t_mmg = T.do(lambda h: h.matmul(b0.ap[0:128, 256:512], onesr[0:1, :],
                                    bsmall[0:1, :].unsqueeze(1).broadcast_to([1, 16, FD]),
                                    start=False, stop=True), inc=te)
    b0.set_written(t_mmg)
    b1.pre_write(T)
    t_mme = T.do(lambda h: h.matmul(b1.ap[0:128, 0:16], onesr[0:1, :], ebrow[0:1, :],
                                    start=True, stop=True), inc=te)
    b1.set_written(t_mme)
    b0.pre_read(V)
    V.do(lambda h: h.tensor_scalar_mul(gamq[:], b0.ap[:, 0:256], 0.25))
    V.do(lambda h: h.tensor_copy(gamk[:], b0.ap[:, 0:256]))
    V.do(lambda h: h.tensor_scalar_mul(betq[:], b0.ap[:, 256:512], 0.25))
    t_vcb = V.do(lambda h: h.tensor_copy(betk[:], b0.ap[:, 256:512]), inc=ve)
    b0.add_reader(t_vcb)
    b1.pre_read(V)
    t_veb = V.do(lambda h: h.tensor_copy(ebias_t[:], b1.ap[:, 0:16]), inc=ve)
    b1.add_reader(t_veb)

    wtoks = [None, None]
    wtoks_dn = [None, None]
    def load_expert(eidx, buf, pretoks):
        G.waits(pretoks)
        t1 = dma(G, g_buf[buf], gusb[buf], gu_d[eidx].rearrange("(c p) n -> p c n", p=128))
        t2 = dma(G, g_dnb[buf], dnsb[buf], dn_d[eidx].rearrange("(c p) n -> p c n", p=128))
        wtoks[buf] = t1
        wtoks_dn[buf] = t2

    zbuf_last_tt = [[]]
    rzdr_bc = [[], []]
    attn_state = {"ostg_free": None, "o_c": None, "zt_free": None, "rz_free": None}

    def emit_zpath(P):
        V.waits(P["zrow"])
        V.wait(attn_state["rz_free"])
        V.do(lambda h: h.tensor_scalar_add(zt[:], zt[:], EPS))
        vsep()
        def _recip(h):
            with nc.allow_low_precision(reason="bf16 reciprocal of z (normalizer)"):
                return h.reciprocal(rz[:], zt[:])
        t_rz = V.do(_recip, inc=ve)
        attn_state["zt_free"] = t_rz
        par = P["par"]
        Y.waits(rzdr_bc[par])
        rzdr_bc[par][:] = []
        Y.wait(t_rz)
        t_zst = dma(Y, s_zb, rz_dr[par], rz[:])
        attn_state["rz_free"] = t_zst
        P["store"] = t_zst
        emit_bcast(P, 0)

    def emit_group(P, g):
        cc = P["cch"]
        V.wait(P["bc"][g])
        if g == 0:
            V.wait(attn_state["ostg_free"])
        last = []
        for p in (2 * g, 2 * g + 1):
            t_ne = V.do(lambda h, p=p, cc=cc: h.tensor_tensor(
                onT[0:64, p, 512 * cc:512 * (cc + 1)], orw[0:64, 2 * p, :],
                zbuf[:, (2 * p) % 4, :], ALU.mult), inc=ve)
            t_no = V.do(lambda h, p=p: h.tensor_tensor(
                ostg[0:64, p, :], orw[0:64, 2 * p + 1, :],
                zbuf[:, (2 * p + 1) % 4, :], ALU.mult), inc=ve)
            P["odd"].append(t_no)
            P["tt_last"] = t_no
            last = [t_ne, t_no]
        zbuf_last_tt[0] = last

    def emit_bcast(P, g):
        Y.waits(zbuf_last_tt[0])
        Y.wait(P["store"])
        tok = None
        for q in range(4):
            tok = dma(Y, s_zb, zbuf[:, q, :],
                      rz_dr[P["par"], 4 * g + q:4 * g + q + 1, :].to_broadcast((64, 512)))
        P["bc"][g] = tok
        rzdr_bc[P["par"]].append(tok)

    def finish_chunk(P):
        cc = P["cch"]
        Y.waits(P["odd"])
        t_sh = dma(Y, s_sh, onT[64:128, :, 512 * cc:512 * (cc + 1)], ostg[0:64, :, :])
        attn_state["ostg_free"] = t_sh
        attn_state["o_c"].append((P["tt_last"], t_sh))

    prev_store_tok = None
    for s in range(SEQN):
        barrier()
        t_wqkv = dma(G, g_wq, wqkv, wqkvT_d.rearrange("(c p) n -> p c n", p=128))
        t_ones = V.do(lambda h: h.memset(vsb[:, :, :, 64:65], 1.0), inc=ve)

        xa_tok = [None, None]
        xa_free = [None, None]
        x1f_free = [None, None]
        x1t_free = None
        qkpad_free = None
        fmsc_free = None
        fmv_free = None
        vsb_toks = []
        t_qev = None

        def p1_head(tt):
            par = tt % 2
            if tt < 2:
                xa_tok[par] = dma(Y, s_xa[par], xa[:, par, :], x_in[s, tt * 128:(tt + 1) * 128, :])
            V.wait(xa_tok[par]); C.wait(xa_tok[par])
            V.do(lambda h, p=par: h.tensor_reduce(lns[:, 0:1], xa[:, p, :], AX.X, ALU.add))
            C.wait(fmv_free)
            t_sq = C.do(lambda h, p=par: h.activation(sq_sc[:], xa[:, p, :], AF.Square,
                                                      accum_out=lns[:, 1:2]), inc=se)
            V.wait(t_sq)
            vsep()
            V.do(lambda h: h.tensor_scalar_mul(lns[:, 2:3], lns[:, 0:1], 1.0 / D))
            vsep()
            V.do(lambda h: h.tensor_tensor(lns[:, 3:4], lns[:, 2:3], lns[:, 2:3], ALU.mult))
            vsep()
            t_var = V.do(lambda h: h.scalar_tensor_tensor(lns[:, 4:5], lns[:, 1:2], 1.0 / D,
                                                          lns[:, 3:4], ALU.mult, ALU.subtract), inc=ve)
            C.wait(t_var)
            t_sdt = C.do(lambda h: h.activation(lns[:, 5:6], lns[:, 4:5], AF.Sqrt,
                                                bias=ceps[:, 0:1]), inc=se)
            V.wait(t_sdt)
            V.do(lambda h: h.reciprocal(lns[:, 6:7], lns[:, 5:6]))
            vsep()
            V.wait(x1f_free[par])
            t_x1 = V.do(lambda h, p=par: h.tensor_scalar(x1f[:, p, :], xa[:, p, :], lns[:, 2:3],
                                                         lns[:, 6:7], ALU.subtract, ALU.mult), inc=ve)
            xa_free[par] = t_x1
            if tt + 2 < NT:
                Y.wait(xa_free[par]); Y.wait(t_sq)
                xa_tok[par] = dma(Y, s_xa[par], xa[:, par, :],
                                  x_in[s, (tt + 2) * 128:(tt + 3) * 128, :])
            return t_x1

        p1_state = {}
        for _t1 in range(NT + 1):
            if _t1 < NT:
                p1_state[_t1] = p1_head(_t1)
            if _t1 == 0:
                continue
            tt = _t1 - 1
            par = tt % 2
            t_x1 = p1_state.pop(tt)
            T.wait(t_x1)
            T.wait(x1t_free)
            t_x1ev = None
            t_tr = None
            for half in range(2):
                tb = banks[3] if half == 0 else banks[4]
                tb.pre_write(T)
                for c4 in range(4):
                    c = half * 4 + c4
                    t_tr = T.do(lambda h, c=c, c4=c4, tb=tb, p=par: h.transpose(
                        tb.ap[:, c4 * 128:(c4 + 1) * 128], x1f[:, p, c * 128:(c + 1) * 128],
                        ident[:]), inc=te)
                tb.set_written(t_tr)
                tb.pre_read(V)
                V.wait(x1t_free)
                t_x1ev = V.do(lambda h, half=half, tb=tb: h.tensor_copy(
                    x1t[:, half * 4:half * 4 + 4, :].rearrange("p c t -> p (c t)"),
                    tb.ap[:]), inc=ve)
                tb.add_reader(t_x1ev)
            x1f_free[par] = t_tr
            t_x1tr_te = t_tr
            qkb, vb0, vb1 = ((banks[0], banks[1], banks[2]) if par == 0
                             else (banks[5], banks[6], banks[7]))
            T.wait(t_x1ev); T.wait(t_wqkv)
            for bnk in (qkb, vb0, vb1):
                bnk.pre_write(T)
            t_mm = None
            for c in range(8):
                for ni, bnk in ((0, qkb), (1, vb0), (2, vb1)):
                    t_mm = T.do(lambda h, c=c, ni=ni, bnk=bnk: h.matmul(
                        bnk.ap[:], x1t[:, c, :], wqkv[:, c, 512 * ni:512 * (ni + 1)],
                        start=(c == 0), stop=(c == 7)),
                        inc=(te if (c == 7 and ni == 2) else None))
            for bnk in (qkb, vb0, vb1):
                bnk.set_written(t_mm)
            x1t_free = t_mm
            V.wait(t_ones)
            vb0.pre_read(V); vb1.pre_read(V)
            V.do(lambda h, tt=tt, vb0=vb0: h.tensor_copy(
                vsb[:, tt, 0:8, 0:64], vb0.ap[:].rearrange("p (h c) -> p h c", h=8)))
            t_vev = V.do(lambda h, tt=tt, vb1=vb1: h.tensor_copy(
                vsb[:, tt, 8:16, 0:64], vb1.ap[:].rearrange("p (h c) -> p h c", h=8)), inc=ve)
            vb0.add_reader(t_vev); vb1.add_reader(t_vev)
            vsb_toks.append(t_vev)
            C.wait(t_x1tr_te)
            fm_toks = []
            t_t1 = None
            t_fsq = None
            for qi in range(2):
                g_t = gamq if qi == 0 else gamk
                b_t = betq if qi == 0 else betk
                off = 256 * qi
                qkb.pre_read(C); qkb.pre_read(V)
                t_fsq = C.do(lambda h, off=off, qkb=qkb: h.activation(
                    sq_sc[:, 0:256], qkb.ap[:, off:off + 256], AF.Square), inc=se)
                V.wait(t_fsq)
                V.do(lambda h, off=off, qkb=qkb: h.tensor_reduce(
                    fms[:, 0, :], qkb.ap[:, off:off + 256].rearrange("p (a b) -> p a b", a=16),
                    AX.X, ALU.add))
                V.do(lambda h: h.tensor_reduce(
                    fms[:, 1, :], sq_sc[:, 0:256].rearrange("p (a b) -> p a b", a=16),
                    AX.X, ALU.add))
                V.do(lambda h: h.tensor_scalar_mul(fms[:, 2, :], fms[:, 0, :], 1.0 / FD))
                vsep()
                V.do(lambda h: h.tensor_tensor(fms[:, 3, :], fms[:, 2, :], fms[:, 2, :], ALU.mult))
                vsep()
                t_fv = V.do(lambda h: h.scalar_tensor_tensor(fms[:, 4, :], fms[:, 1, :], 1.0 / FD,
                                                             fms[:, 3, :], ALU.mult, ALU.subtract),
                            inc=ve)
                fmv_free = t_fv
                C.wait(t_fv)
                t_fsd = C.do(lambda h: h.activation(fms[:, 5, :], fms[:, 4, :], AF.Sqrt,
                                                    bias=ceps[:, 0:1]), inc=se)
                V.wait(t_fsd)
                V.do(lambda h: h.reciprocal(fms[:, 6, :], fms[:, 5, :]))
                V.wait(fmsc_free)
                vsep()
                V.do(lambda h, g_t=g_t: h.tensor_tensor(
                    fmsc[:, 0, :].rearrange("p (a b) -> p a b", a=16),
                    fms[:, 6, :].unsqueeze(2).broadcast_to([128, 16, FD]),
                    g_t[:].rearrange("p (a b) -> p a b", a=16), ALU.mult))
                t_t1 = V.do(lambda h, off=off, qkb=qkb: h.tensor_tensor(
                    fmsc[:, 1, :].rearrange("p (a b) -> p a b", a=16),
                    qkb.ap[:, off:off + 256].rearrange("p (a b) -> p a b", a=16),
                    fms[:, 2, :].unsqueeze(2).broadcast_to([128, 16, FD]),
                    ALU.subtract), inc=ve)
                G.wait(t_t1)
                G.wait(qkpad_free)
                G.do(lambda h: h.tensor_tensor(
                    fmsc[:, 1, :], fmsc[:, 1, :], fmsc[:, 0, :], ALU.mult))
                t_pad = G.do(lambda h, qi=qi, b_t=b_t: h.tensor_tensor(
                    qkpad[:, qi, :].rearrange("p (c g f) -> p c g f", c=4, g=4)[:, :, :, 0:16],
                    fmsc[:, 1, :].rearrange("p (c g b) -> p c g b", c=4, g=4),
                    b_t[:].rearrange("p (c g b) -> p c g b", c=4, g=4), ALU.add), inc=gc)
                fm_toks.append(t_pad)
                fmsc_free = t_pad
            fmsc_free = fm_toks[-1]
            qkb.add_reader(t_t1); qkb.add_reader(t_fsq)
            T.waits(fm_toks)
            t_tr2 = None
            for qi in range(2):
                dstT = qT if qi == 0 else kT
                tb = banks[3] if qi == 0 else banks[4]
                tb.pre_write(T)
                for ci in range(4):
                    t_tr2 = T.do(lambda h, qi=qi, ci=ci, tb=tb: h.transpose(
                        tb.ap[:, ci * 128:(ci + 1) * 128],
                        qkpad[:, qi, ci * 128:(ci + 1) * 128], ident[:]), inc=te)
                tb.set_written(t_tr2)
                tb.pre_read(V)
                t_qev = V.do(lambda h, tt=tt, dstT=dstT, tb=tb: h.tensor_copy(
                    dstT[:, :, tt * 128:(tt + 1) * 128],
                    tb.ap[:].rearrange("p (c t) -> p c t", c=4)), inc=ve)
                tb.add_reader(t_qev)
            qkpad_free = t_tr2

        t_pass1_v = t_qev

        if debug >= 1 and s == 0:
            G.wait(t_pass1_v); G.waits(vsb_toks)
            dma(G, g_dbg, dbg["x1"], sq_sc[:])
            dma(G, g_dbg, dbg["lns"], lns[:])
            dma(G, g_dbg, dbg["x1t"], x1t[:].rearrange("p c t -> p (c t)"))
            dma(G, g_dbg, dbg["qT"], qT.rearrange("p c t -> p (c t)"))
            dma(G, g_dbg, dbg["kT"], kT.rearrange("p c t -> p (c t)"))
            dma(G, g_dbg, dbg["v"], vsb.rearrange("p t h c -> p (t h c)"))

        if stage < 2:
            continue
        o_c_toks = []
        st2_rd = [None, None]
        zd_prev = [None, None, None, None]
        pending = None
        attn_state["o_c"] = o_c_toks
        for cch in range(NCH):
            njt = 4 * cch + 4
            zrow_dma_toks = []
            for w in range(4):
                svb = [banks[4 + q] for q in range(4)]
                qkbk = [banks[q] for q in range(4)]
                t_sv = None
                pend = None
                def emit_sv(jinfo, is_first, is_last):
                    j, jp, st_col, toks4 = jinfo
                    T.waits(toks4)
                    T.wait(vsb_toks[j])
                    t_sv_ = None
                    for q in range(4):
                        if is_first:
                            svb[q].pre_write(T)
                        hh = 4 * w + q
                        t_sv_ = T.do(lambda h, q=q, hh=hh, j=j, jp=jp, st=st_col, svb=svb,
                                     st_=is_first, sp_=is_last: h.matmul(
                            svb[q].ap[0:65, st:512], vsb[:, j, hh, 0:65], st2[:, jp, q, st:512],
                            start=st_, stop=sp_),
                            inc=(te if q == 3 else None))
                    return t_sv_
                for j in range(njt):
                    jp = j % 2
                    diag = j >= 4 * cch
                    st_col = 128 * (j - 4 * cch) if diag else 0
                    T.wait(t_pass1_v)
                    for q in range(4):
                        qkbk[q].pre_write(T)
                    t_qk = None
                    for q in range(4):
                        for m in range(4):
                            t_qk = T.do(lambda h, q=q, m=m, w=w, j=j, st=st_col, cch=cch: h.matmul(
                                qkbk[q].ap[32 * m:32 * m + 32, st:512],
                                kT[32 * q:32 * q + 16, w, j * 128 + 32 * m: j * 128 + 32 * m + 32],
                                qT[32 * q:32 * q + 16, w, 512 * cch + st: 512 * (cch + 1)],
                                tile_position=(32 * q, 32 * m)),
                                inc=(te if (q == 3 and m == 3) else None))
                    for q in range(4):
                        qkbk[q].set_written(t_qk)
                    toks4 = []
                    for q in (0, 1, 2):
                        qkbk[q].pre_read(C)
                        C.wait(st2_rd[jp])
                        t_e = C.do(lambda h, q=q, jp=jp, st=st_col, b=qkbk[q]: h.activation(
                            st2[:, jp, q, st:512], b.ap[:, st:512], AF.Square), inc=se)
                        qkbk[q].add_reader(t_e)
                        toks4.append(t_e)
                    qkbk[3].pre_read(V)
                    V.wait(st2_rd[jp])
                    t_cp = V.do(lambda h, jp=jp, st=st_col, b=qkbk[3]: h.tensor_copy(
                        st2[:, jp, 3, st:512], b.ap[:, st:512]), inc=ve)
                    qkbk[3].add_reader(t_cp)
                    G.wait(t_cp)
                    mt = G.do(lambda h, jp=jp, st=st_col: h.tensor_tensor(
                        st2[:, jp, 3, st:512], st2[:, jp, 3, st:512],
                        st2[:, jp, 3, st:512], ALU.mult), inc=(gc if not diag else None))
                    if diag:
                        V.waits(toks4)
                        t_mv = None
                        for q in (0, 1, 2):
                            t_mv = V.do(lambda h, q=q, jp=jp, st=st_col: h.tensor_tensor(
                                st2[:, jp, q, st:st + 128], st2[:, jp, q, st:st + 128],
                                tril[:], ALU.mult), inc=(ve if q == 2 else None))
                        mt = G.do(lambda h, jp=jp, st=st_col: h.tensor_tensor(
                            st2[:, jp, 3, st:st + 128], st2[:, jp, 3, st:st + 128],
                            tril[:], ALU.mult), inc=gc)
                        toks4 = [t_mv, mt]
                    else:
                        toks4.append(mt)
                    if pend is not None:
                        t_sv = emit_sv(pend, pend[0] == 0, False)
                        st2_rd[pend[1]] = t_sv
                    pend = (j, jp, st_col, toks4)
                t_sv = emit_sv(pend, pend[0] == 0, True)
                st2_rd[pend[1]] = t_sv
                pend = None
                for q in range(4):
                    svb[q].set_written(t_sv)
                if pending is not None:
                    if w == 0:
                        emit_zpath(pending)
                    emit_group(pending, w)
                    if w <= 2:
                        emit_bcast(pending, w + 1)
                    if w == 3:
                        finish_chunk(pending)
                for q in range(4):
                    svb[q].pre_read(V)
                V.wait(zd_prev[w])
                t_or = V.do(lambda h, w=w: h.tensor_copy(
                    orw[0:65, 4 * w:4 * w + 4, :],
                    PA[0:65, 2048:4096].rearrange("p (h t) -> p h t", h=4)), inc=ve)
                for q in range(4):
                    svb[q].add_reader(t_or)
                G.wait(t_or)
                G.wait(attn_state["zt_free"])
                t_zd = dma(G, g_z, zt[4 * w:4 * w + 4, :], orw[64:65, 4 * w:4 * w + 4, :])
                zrow_dma_toks.append(t_zd)
            zd_prev = zrow_dma_toks
            pending = {"cch": cch, "par": cch % 2, "zrow": zrow_dma_toks,
                       "store": None, "bc": {}, "tt_last": None, "odd": []}
        emit_zpath(pending)
        emit_group(pending, 0)
        emit_bcast(pending, 1)
        emit_group(pending, 1)
        emit_bcast(pending, 2)
        emit_group(pending, 2)
        emit_bcast(pending, 3)
        emit_group(pending, 3)
        finish_chunk(pending)
        pending = None

        if debug >= 2 and s == 0:
            for pt in o_c_toks:
                G.waits(list(pt))
            dma(G, g_dbg, dbg["onT"], onT.rearrange("p c t -> p (c t)"))

        for pt in o_c_toks:
            G.waits(list(pt))
        t_woT = dma(G, g_wo, woT, woT_d.rearrange("(c p) n -> p c n", p=128))

        if stage < 3:
            continue
        t_sh_all = s_sh.tok()
        xr_tok = [None, None]
        xr_free = [None, None]
        x2t_toks = []
        gts_free = None
        t_wt = None
        sq2_free = None
        t_wo = None
        def p2_head(tt):
            par = tt % 2
            if tt < 2:
                Y.wait(fmsc_free)
                Y.wait(qkpad_free)
                xr_tok[par] = dma(Y, s_xa[par], xr[:, par, :], x_in[s, tt * 128:(tt + 1) * 128, :])
            cch = tt // 4
            wob = (banks[2], banks[3]) if par == 0 else (banks[4], banks[5])
            T.wait(o_c_toks[cch][0])
            T.wait(t_sh_all)
            T.wait(t_woT)
            for bnk in wob:
                bnk.pre_write(T)
            t_wo = None
            for c in range(8):
                for ni in range(2):
                    t_wo = T.do(lambda h, c=c, ni=ni, tt=tt, wob=wob: h.matmul(
                        wob[ni].ap[:], onT[:, c, tt * 128:(tt + 1) * 128],
                        woT[:, c, 512 * ni:512 * (ni + 1)],
                        start=(c == 0), stop=(c == 7)),
                        inc=(te if (c == 7 and ni == 1) else None))
            for bnk in wob:
                bnk.set_written(t_wo)
            V.wait(xr_tok[par])
            wob[0].pre_read(V); wob[1].pre_read(V)
            V.do(lambda h, tt=tt, p=par, wob=wob: h.tensor_tensor(
                ysb[:, tt, 0:512], wob[0].ap[:], xr[:, p, 0:512], ALU.add))
            t_res = V.do(lambda h, tt=tt, p=par, wob=wob: h.tensor_tensor(
                ysb[:, tt, 512:1024], wob[1].ap[:], xr[:, p, 512:1024], ALU.add), inc=ve)
            wob[0].add_reader(t_res); wob[1].add_reader(t_res)
            xr_free[par] = t_res
            if tt + 2 < NT:
                Y.wait(xr_free[par])
                xr_tok[par] = dma(Y, s_xa[par], xr[:, par, :], x_in[s, (tt + 2) * 128:(tt + 3) * 128, :])
            return t_res, t_wo

        p2_state = {}
        for tt in range(NT + 1):
            if tt < NT:
                p2_state[tt] = p2_head(tt)
            if tt == 0:
                continue
            tt = tt - 1
            par = tt % 2
            t_res, t_wo = p2_state.pop(tt)
            if tt == NT - 1:
                t_wo_last_v = t_wo
            C.wait(t_res)
            C.wait(sq2_free)
            C.wait(gts_free)
            vsep()
            V.do(lambda h, tt=tt: h.tensor_reduce(lns[:, 0:1], ysb[:, tt, :], AX.X, ALU.add))
            t_sq2 = C.do(lambda h, tt=tt: h.activation(sq_sc[:], ysb[:, tt, :], AF.Square,
                                                       accum_out=lns[:, 1:2]), inc=se)
            V.wait(t_sq2)
            vsep()
            V.do(lambda h: h.tensor_scalar_mul(lns[:, 2:3], lns[:, 0:1], 1.0 / D))
            vsep()
            V.do(lambda h: h.tensor_tensor(lns[:, 3:4], lns[:, 2:3], lns[:, 2:3], ALU.mult))
            vsep()
            t_var2 = V.do(lambda h: h.scalar_tensor_tensor(lns[:, 4:5], lns[:, 1:2], 1.0 / D,
                                                           lns[:, 3:4], ALU.mult, ALU.subtract), inc=ve)
            C.wait(t_var2)
            t_sd2 = C.do(lambda h: h.activation(lns[:, 5:6], lns[:, 4:5], AF.Sqrt,
                                                bias=ceps[:, 0:1]), inc=se)
            V.wait(t_sd2)
            V.do(lambda h: h.reciprocal(lns[:, 6:7], lns[:, 5:6]))
            vsep()
            t_x2 = V.do(lambda h, tt=tt: h.tensor_scalar(sq_sc[:], ysb[:, tt, :], lns[:, 2:3],
                                                         lns[:, 6:7], ALU.subtract, ALU.mult), inc=ve)
            T.wait(t_x2)
            t_x2tv = None
            t_tr3 = None
            for half in range(2):
                tb = banks[6] if half == 0 else banks[7]
                tb.pre_write(T)
                for c4 in range(4):
                    c = half * 4 + c4
                    t_tr3 = T.do(lambda h, c=c, c4=c4, tb=tb: h.transpose(
                        tb.ap[:, c4 * 128:(c4 + 1) * 128], sq_sc[:, c * 128:(c + 1) * 128],
                        ident[:]), inc=te)
                tb.set_written(t_tr3)
                tb.pre_read(V)
                t_x2tv = V.do(lambda h, tt=tt, half=half, tb=tb: h.tensor_copy(
                    x2t[:, half * 4:half * 4 + 4, tt * 128:(tt + 1) * 128],
                    tb.ap[:].rearrange("p (c t) -> p c t", c=4)), inc=ve)
                tb.add_reader(t_x2tv)
            sq2_free = t_tr3
            x2t_toks.append(t_x2tv)
            lgb = banks[0] if par == 0 else banks[1]
            T.wait(t_x2tv); T.wait(t_gwT)
            lgb.pre_write(T)
            t_lg = None
            for c in range(8):
                t_lg = T.do(lambda h, c=c, tt=tt, lgb=lgb: h.matmul(
                    lgb.ap[:, 0:16], x2t[:, c, tt * 128:(tt + 1) * 128], gwT[:, c, :],
                    start=(c == 0), stop=(c == 7)), inc=(te if c == 7 else None))
            lgb.set_written(t_lg)
            lgb.pre_read(C)
            C.wait(gts_free)
            t_sc = C.do(lambda h, lgb=lgb: h.activation(gts[:, 0:16], lgb.ap[:, 0:16],
                                                        AF.Sigmoid), inc=se)
            lgb.add_reader(t_sc)
            V.wait(t_sc); V.wait(t_veb)
            g = gts
            V.do(lambda h: h.tensor_tensor(g[:, 16:32], g[:, 0:16], ebias_t[:], ALU.add))
            pr = [(0, 1), (0, 2), (0, 3), (1, 2), (1, 3), (2, 3)]
            vsep()
            for i, (a, b_) in enumerate(pr):
                V.do(lambda h, a=a, b_=b_, i=i: h.tensor_tensor(
                    g[:, 32 + 4 * i:36 + 4 * i].unsqueeze(2),
                    g[:, 16:32].rearrange("p (x y) -> p x y", y=4)[:, :, a:a + 1],
                    g[:, 16:32].rearrange("p (x y) -> p x y", y=4)[:, :, b_:b_ + 1], ALU.add))
            V.do(lambda h: h.tensor_tensor(g[:, 56:60], g[:, 32:36], g[:, 36:40], ALU.max))
            V.do(lambda h: h.tensor_tensor(g[:, 60:64], g[:, 40:44], g[:, 44:48], ALU.max))
            V.do(lambda h: h.tensor_tensor(g[:, 64:68], g[:, 48:52], g[:, 52:56], ALU.max))
            V.do(lambda h: h.tensor_tensor(g[:, 56:60], g[:, 56:60], g[:, 60:64], ALU.max))
            vsep()
            V.do(lambda h: h.tensor_tensor(g[:, 56:60], g[:, 56:60], g[:, 64:68], ALU.max))
            vsep()
            V.do(lambda h: h.tensor_reduce(lns[:, 0:1], g[:, 56:60], AX.X, ALU.max))
            vsep()
            V.do(lambda h: h.tensor_scalar(g[:, 60:64], g[:, 56:60], lns[:, 0:1], None, ALU.is_ge))
            vsep()
            V.do(lambda h: h.scalar_tensor_tensor(g[:, 60:64], g[:, 60:64], -1e30,
                                                  g[:, 56:60], ALU.mult, ALU.add))
            vsep()
            V.do(lambda h: h.tensor_reduce(lns[:, 1:2], g[:, 60:64], AX.X, ALU.max))
            vsep()
            V.do(lambda h: h.tensor_scalar(g[:, 60:64], g[:, 56:60], lns[:, 1:2], None, ALU.is_ge))
            vsep()
            V.do(lambda h: h.tensor_tensor(
                g[:, 32:48].rearrange("p (x y) -> p x y", y=4),
                g[:, 16:32].rearrange("p (x y) -> p x y", y=4),
                g[:, 60:64].unsqueeze(2).broadcast_to([128, 4, 4]), ALU.mult))
            vsep()
            V.do(lambda h: h.tensor_copy(g[:, 64:80], g[:, 32:48]))
            vsep()
            for r in range(3):
                V.do(lambda h: h.tensor_reduce(lns[:, 0:1], g[:, 64:80], AX.X, ALU.max))
                vsep()
                V.do(lambda h: h.tensor_scalar(g[:, 48:64], g[:, 64:80], lns[:, 0:1], None,
                                               ALU.is_ge))
                vsep()
                V.do(lambda h: h.scalar_tensor_tensor(g[:, 64:80], g[:, 48:64], -1e30,
                                                      g[:, 64:80], ALU.mult, ALU.add))
                vsep()
            V.do(lambda h: h.tensor_reduce(lns[:, 0:1], g[:, 64:80], AX.X, ALU.max))
            vsep()
            V.do(lambda h: h.tensor_scalar(g[:, 48:64], g[:, 32:48], lns[:, 0:1], None, ALU.is_ge))
            vsep()
            V.do(lambda h: h.tensor_tensor(g[:, 64:80], g[:, 0:16], g[:, 48:64], ALU.mult))
            vsep()
            V.do(lambda h: h.tensor_reduce(lns[:, 0:1], g[:, 64:80], AX.X, ALU.add))
            vsep()
            V.do(lambda h: h.tensor_scalar_add(lns[:, 0:1], lns[:, 0:1], 1e-20))
            vsep()
            V.do(lambda h: h.reciprocal(lns[:, 1:2], lns[:, 0:1]))
            vsep()
            t_W = V.do(lambda h: h.tensor_scalar(g[:, 48:64], g[:, 64:80], lns[:, 1:2], 2.5,
                                                 ALU.mult, ALU.mult), inc=ve)
            gts_free = t_W
            tbW = banks[6]
            T.wait(t_W)
            tbW.pre_write(T)
            t_trW = T.do(lambda h, tbW=tbW: h.transpose(tbW.ap[0:16, 0:128], g[:, 48:64],
                                                        ident[:]), inc=te)
            tbW.set_written(t_trW)
            tbW.pre_read(V)
            t_wt = V.do(lambda h, tt=tt, tbW=tbW: h.tensor_copy(
                wt[0:16, tt * 128:(tt + 1) * 128], tbW.ap[0:16, 0:128]), inc=ve)
            tbW.add_reader(t_wt)

        t_wo_last = t_wo_last_v
        if debug >= 3 and s == 0:
            G.wait(t_wt)
            G.waits(x2t_toks)
            dma(G, g_dbg, dbg["x2"], ysb.rearrange("p t d -> p (t d)"))
            dma(G, g_dbg, dbg["x2t"], x2t[:].rearrange("p c t -> p (c t)"))
            t_dbg3 = dma(G, g_dbg, dbg["W"], wt[:])
            V.wait(t_dbg3)
            T.wait(t_dbg3)

        if stage < 4:
            continue
        t_x2t_all = x2t_toks[-1]
        t_wt_all = t_wt
        groups = [[0, 1], [2, 3], [4, 5], [6, 7], [8, 9], [10, 11],
                  [12, 13], [14, 15], [16]]
        NOWEB = 0
        PHB = 3
        eorder = [e for grp in groups for e in grp]
        load_expert(eorder[0], 0, [t_wo_last])
        load_expert(eorder[1], 1, [t_wo_last])
        yev_tok = None
        intT_free = [None, None]
        web_free = None
        sg_free = None
        t1_free = None
        store_toks = []
        exp_final_tok = [None] * len(eorder)
        eidx_flat = 0
        for gi, grp in enumerate(groups):
            for t256 in range(8):
                ybanks = [banks[i] for i in range(4)]
                for bnk in ybanks:
                    bnk.pre_write(T)
                t_dn = None
                for ei, e in enumerate(grp):
                    flat = eidx_flat + ei
                    buf = flat % 2
                    is_shared = (e == 16) or NOWEB
                    T.wait(wtoks[buf])
                    if PHB == 0:
                        t_dn = None
                        continue
                    t_wev = None
                    if not is_shared:
                        wb = banks[6]
                        wb.pre_write(T)
                        T.wait(t_wt_all)
                        T.do(lambda h, e=e, t256=t256, wb=wb: h.matmul(
                            wb.ap[0:64, 0:256], selm[0:16, e * 64:(e + 1) * 64],
                            wt[0:16, 256 * t256:256 * (t256 + 1)],
                            start=True, stop=True))
                        t_web = T.do(lambda h, e=e, t256=t256, wb=wb: h.matmul(
                            wb.ap[64:128, 0:256], selm[0:16, e * 64:(e + 1) * 64],
                            wt[0:16, 256 * t256:256 * (t256 + 1)],
                            start=True, stop=True, tile_position=(0, 64)), inc=te)
                        wb.set_written(t_web)
                        wb.pre_read(C)
                        C.wait(web_free)
                        t_wev = C.do(lambda h, wb=wb: h.activation(
                            websb[:], wb.ap[:, 0:256], AF.Identity), inc=se)
                        wb.add_reader(t_wev)
                    def emit_gu(ic):
                        gub = banks[4 + (ic % 2)]
                        gub.pre_write(T)
                        T.wait(t_x2t_all)
                        t_gu = None
                        for c in range(8):
                            for half in range(2):
                                t_gu = T.do(lambda h, c=c, ic=ic, half=half, t256=t256,
                                            buf=buf, gub=gub: h.matmul(
                                    gub.ap[:, 256 * half:256 * half + 256],
                                    gusb[buf][:, c, 512 * half + 128 * ic: 512 * half + 128 * ic + 128],
                                    x2t[:, c, 256 * t256:256 * (t256 + 1)],
                                    start=(c == 0 and half == 0), stop=(c == 7 and half == 1)),
                                    inc=(te if (c == 7 and half == 1) else None))
                        gub.set_written(t_gu)
                        return gub
                    gubs = {0: emit_gu(0), 1: emit_gu(1)}
                    t_dn = None
                    for ic in range(4):
                        gub = gubs.pop(ic)
                        if PHB == 1:
                            if ic + 2 < 4:
                                gubs[ic + 2] = emit_gu(ic + 2)
                            continue
                        gub.pre_read(C)
                        C.wait(sg_free)
                        t_sg = C.do(lambda h, gub=gub: h.activation(
                            sgsc[:], gub.ap[:, 0:256], AF.Silu), inc=se)
                        gub.pre_read(V)
                        V.wait(t_sg)
                        V.wait(t1_free)
                        if not is_shared:
                            V.wait(t_wev)
                            t_t1m = V.do(lambda h, gub=gub: h.tensor_tensor(
                                t1sc[:], gub.ap[:, 256:512], websb[:], ALU.mult), inc=ve)
                        else:
                            t_t1m = V.do(lambda h, gub=gub: h.tensor_copy(
                                t1sc[:], gub.ap[:, 256:512]), inc=ve)
                        gub.add_reader(t_sg); gub.add_reader(t_t1m)
                        V.wait(intT_free[ic % 2])
                        vsep()
                        t_int = V.do(lambda h, ic=ic: h.tensor_tensor(
                            intT[:, ic % 2, :], t1sc[:], sgsc[:], ALU.mult), inc=ve)
                        sg_free = t_int
                        t1_free = t_int
                        if not is_shared:
                            web_free = t_int
                        if ic + 2 < 4:
                            gubs[ic + 2] = emit_gu(ic + 2)
                        if PHB == 2:
                            continue
                        T.wait(t_int)
                        T.wait(wtoks_dn[buf])
                        first = (ei == 0 and ic == 0)
                        last = (ei == len(grp) - 1 and ic == 3)
                        for tth in range(2):
                            T.do(lambda h, ic=ic, tth=tth, buf=buf, yb=ybanks[2 * tth],
                                 first=first, last=last: h.matmul(
                                yb.ap[:], intT[:, ic % 2, 128 * tth:128 * tth + 128],
                                dnsb[buf][:, ic, 0:512],
                                start=first, stop=last))
                            t_dn = T.do(lambda h, ic=ic, tth=tth, buf=buf, yb=ybanks[2 * tth + 1],
                                        first=first, last=last: h.matmul(
                                yb.ap[:], intT[:, ic % 2, 128 * tth:128 * tth + 128],
                                dnsb[buf][:, ic, 512:1024],
                                start=first, stop=last),
                                inc=(te if tth == 1 else None))
                        intT_free[ic % 2] = t_dn
                    if t256 == 7:
                        exp_final_tok[flat] = t_dn
                for bnk in ybanks:
                    bnk.set_written(t_dn)
                if PHB < 3:
                    continue
                for tth in range(2):
                    tt = 2 * t256 + tth
                    yb0, yb1 = ybanks[2 * tth], ybanks[2 * tth + 1]
                    yb0.pre_read(V); yb1.pre_read(V)
                    V.do(lambda h, tt=tt, yb=yb0: h.tensor_tensor(
                        ysb[:, tt, 0:512], yb.ap[:], ysb[:, tt, 0:512], ALU.add))
                    yev_tok = V.do(lambda h, tt=tt, yb=yb1: h.tensor_tensor(
                        ysb[:, tt, 512:1024], yb.ap[:], ysb[:, tt, 512:1024], ALU.add), inc=ve)
                    yb0.add_reader(yev_tok); yb1.add_reader(yev_tok)
                    if gi == len(groups) - 1:
                        Y.wait(yev_tok)
                        store_toks.append(dma(Y, s_st, y_out[s, tt * 128:(tt + 1) * 128, :],
                                              ysb[:, tt, :]))
            for ei in range(len(grp)):
                flat = eidx_flat + ei
                nxt = flat + 2
                if nxt < len(eorder):
                    load_expert(eorder[nxt], nxt % 2, [exp_final_tok[flat]])
            eidx_flat += len(grp)

        if not store_toks:
            for tt in range(NT):
                Y.wait(yev_tok or te.tok())
                store_toks.append(dma(Y, s_st, y_out[s, tt * 128:(tt + 1) * 128, :],
                                      ysb[:, tt, :]))
        prev_store_tok = store_toks[-1] if store_toks else None

    for sset in DMAS:
        Y.wait(sset.tok())

    with nc.Block() as block:
        @block.sync
        def _(h):
            for f in Y.ops:
                f(h)
        @block.gpsimd
        def _(h):
            for f in G.ops:
                f(h)
        @block.vector
        def _(h):
            for f in V.ops:
                f(h)
        @block.scalar
        def _(h):
            for f in C.ops:
                f(h)
        @block.tensor
        def _(h):
            for f in T.ops:
                f(h)

    return nc


def make_inputs(ins, core):
    x = np.ascontiguousarray(np.asarray(ins["L0_attn_x"])[2 * core:2 * core + 2])
    wqkvT = np.ascontiguousarray(
        np.concatenate([np.asarray(ins["L0_attn_Wq"]).T, np.asarray(ins["L0_attn_Wk"]).T,
                        np.asarray(ins["L0_attn_Wv"]).T], axis=1))
    woT = np.ascontiguousarray(np.asarray(ins["L0_attn_Wo"]).T)
    gp = np.asarray(ins["L0_ffn_gate_proj"])
    up = np.asarray(ins["L0_ffn_up_proj"])
    dp = np.asarray(ins["L0_ffn_down_proj"])
    gu = np.concatenate([np.transpose(gp, (0, 2, 1)), np.transpose(up, (0, 2, 1))], axis=2)
    shgu = np.concatenate([np.asarray(ins["L0_ffn_shared_gate_w"]).T,
                           np.asarray(ins["L0_ffn_shared_up_w"]).T], axis=1)[None]
    gu = np.concatenate([gu, shgu], axis=0)
    dn = np.transpose(dp, (0, 2, 1))
    dn = np.concatenate([dn, np.asarray(ins["L0_ffn_shared_down_w"]).T[None]], axis=0)
    return {
        "x": x.astype(np.float32),
        "wqkvT": wqkvT.astype(np.float32),
        "woT": woT.astype(np.float32),
        "gu": np.ascontiguousarray(gu, dtype=np.float32),
        "dn": np.ascontiguousarray(dn, dtype=np.float32),
        "gwT_in": np.ascontiguousarray(np.asarray(ins["L0_ffn_gate_w"]).T, dtype=np.float32),
        "gam": np.asarray(ins["L0_attn_gamma"], dtype=np.float32).reshape(1, FD),
        "bet": np.asarray(ins["L0_attn_beta"], dtype=np.float32).reshape(1, FD),
        "ebias": np.asarray(ins["L0_ffn_e_bias"], dtype=np.float32).reshape(1, NE),
    }


def kernel(**inputs):
    nc = build_program(debug=0)
    in_maps = [make_inputs(inputs, c) for c in range(8)]
    res = run_bass_kernel_spmd(nc, in_maps, core_ids=list(range(8)))
    out = np.concatenate([np.asarray(r["y"]) for r in res.results], axis=0)
    return out.reshape(B, S, D)


# version 08
# speedup vs baseline: 1.0431x; 1.0431x over previous
import numpy as np
import concourse.bass as bass
import concourse.mybir as mybir
from concourse.bass_utils import run_bass_kernel_spmd

F32 = mybir.dt.float32
BF16 = mybir.dt.bfloat16
U8 = mybir.dt.uint8
AF = mybir.ActivationFunctionType
ALU = mybir.AluOpType
AX = mybir.AxisListType

B, S, D = 16, 2048, 1024
H, FD, HD = 16, 16, 64
NE, DI = 16, 512
NT = S // 128
NCH = 4
SEQN = 2
EPS = 1e-5

class Tok:
    __slots__ = ("sem", "val")
    def __init__(self, sem, val):
        self.sem, self.val = sem, val

class SemC:
    def __init__(self, nc, name):
        self.h = nc.alloc_semaphore(name=name)
        self.n = 0
    def tok(self):
        return Tok(self, self.n)

class EngQ:
    def __init__(self, name):
        self.name = name
        self.ops = []
        self.mark = {}
    def wait(self, tok):
        if tok is None:
            return
        k = id(tok.sem)
        if self.mark.get(k, 0) >= tok.val:
            return
        self.mark[k] = tok.val
        sem_h, v = tok.sem.h, tok.val
        self.ops.append(lambda h: h.wait_ge(sem_h, v))
    def waits(self, toks):
        for t in toks:
            self.wait(t)
    def do(self, fn, inc=None, n=1):
        if inc is not None:
            inc.n += n
            sem_h, nn = inc.h, n
            def f(h, fn=fn):
                fn(h).then_inc(sem_h, nn)
            self.ops.append(f)
            return inc.tok()
        self.ops.append(lambda h, fn=fn: fn(h))
        return None

class Bank:
    def __init__(self, ap):
        self.ap = ap
        self.wtok = None
        self.rtoks = []
    def pre_write(self, eng):
        eng.waits(self.rtoks)
    def set_written(self, tok):
        self.wtok = tok
        self.rtoks = []
    def pre_read(self, eng):
        eng.wait(self.wtok)
    def add_reader(self, tok):
        if tok is not None:
            self.rtoks.append(tok)


def build_program(debug=0, stage=4):
    nc = bass.Bass("TRN2", target_bir_lowering=False, detect_race_conditions=False)

    x_in = nc.dram_tensor("x", [SEQN, S, D], F32, kind="ExternalInput").ap()
    wqkvT_d = nc.dram_tensor("wqkvT", [D, 1536], F32, kind="ExternalInput").ap()
    woT_d = nc.dram_tensor("woT", [D, D], F32, kind="ExternalInput").ap()
    gu_d = nc.dram_tensor("gu", [NE + 1, D, 2 * DI], F32, kind="ExternalInput").ap()
    dn_d = nc.dram_tensor("dn", [NE + 1, DI, D], F32, kind="ExternalInput").ap()
    gwT_d = nc.dram_tensor("gwT_in", [D, NE], F32, kind="ExternalInput").ap()
    gam_d = nc.dram_tensor("gam", [1, FD], F32, kind="ExternalInput").ap()
    bet_d = nc.dram_tensor("bet", [1, FD], F32, kind="ExternalInput").ap()
    ebias_d = nc.dram_tensor("ebias", [1, NE], F32, kind="ExternalInput").ap()
    y_out = nc.dram_tensor("y", [SEQN, S, D], F32, kind="ExternalOutput").ap()
    rz_dr = nc.dram_tensor("rz_scr", [2, 16, 512], BF16, kind="Internal").ap()

    dbg = {}
    def dbgt(name, shape):
        dbg[name] = nc.dram_tensor("d_" + name, shape, F32, kind="ExternalOutput").ap()
    if debug:
        dbgt("x1", [128, D]); dbgt("lns", [128, 8]); dbgt("x1t", [128, 8 * 128])
        dbgt("qT", [128, 4 * S]); dbgt("kT", [128, 4 * S]); dbgt("v", [128, NT * H * 65])
        dbgt("onT", [128, 8 * S]); dbgt("x2", [128, NT * D]); dbgt("W", [16, S])
        dbgt("x2t", [128, 8 * S])

    def sbuf(name, shape, dtype):
        return nc.alloc_sbuf_tensor(name, shape, dtype)

    xa = sbuf("xa", [128, 2, D], F32)
    xrq = sbuf("xrq", [128, 2, D], F32)
    sq_sc = sbuf("sq_sc", [128, D], F32)
    x1t = sbuf("x1t", [128, 8, 128], BF16)
    x2t = sbuf("x2t", [128, 8, S], BF16)
    st2 = sbuf("st2", [128, 2, 4, 512], BF16)
    gwT = sbuf("gwT", [128, 8, 16], BF16)
    wt = sbuf("wt", [16, S], BF16)
    zt = sbuf("zt", [16, 512], F32)
    rz = sbuf("rz", [16, 512], BF16)
    zbuf = sbuf("zbuf", [64, 4, 512], BF16)
    selm = sbuf("selm", [16, 1024], BF16)
    selk = sbuf("selk", [16, 1], F32)
    onesr = sbuf("onesr", [1, 128], F32)
    ident = sbuf("ident", [128, 128], F32)
    tril = sbuf("tril", [128, 128], BF16)
    gamq = sbuf("gamq", [128, 256], F32)
    gamk = sbuf("gamk", [128, 256], F32)
    betq = sbuf("betq", [128, 256], F32)
    betk = sbuf("betk", [128, 256], F32)
    ebias_t = sbuf("ebias_t", [128, 16], F32)
    ceps = sbuf("ceps", [128, 1], F32)
    gsmall = sbuf("gsmall", [1, FD], F32)
    bsmall = sbuf("bsmall", [1, FD], F32)
    ebrow = sbuf("ebrow", [1, NE], F32)
    lns = sbuf("lns", [128, 8], F32)
    fms = sbuf("fms", [128, 8, 16], F32)
    gts = sbuf("gts", [128, 80], F32)
    intT = sbuf("intT", [128, 2, 256], BF16)
    sgsc = sbuf("sgsc", [128, 256], F32)
    t1sc = sbuf("t1sc", [128, 256], F32)
    websb = sbuf("websb", [128, 256], BF16)
    junkv = sbuf("junkv", [128, 8], F32)

    qkpad = xrq[:].rearrange("p a d -> p (a d)")[:, 0:1024].rearrange("p (a b) -> p a b", a=2)
    fmsc = xrq[:].rearrange("p a d -> p (a d)")[:, 1024:1536].rearrange("p (a b) -> p a b", a=2)
    xr = xrq

    ar1 = sbuf("ar1", [128, 66048], U8)
    qT = ar1[:, 0:16384].bitcast(BF16).rearrange("p (c t) -> p c t", c=4)
    kT = ar1[:, 16384:32768].bitcast(BF16).rearrange("p (c t) -> p c t", c=4)
    vsb = ar1[:, 32768:66048].bitcast(BF16).rearrange("p (t h c) -> p t h c", t=NT, h=H)
    ysb = ar1[:, 0:65536].bitcast(F32).rearrange("p (t d) -> p t d", t=NT)

    ar2 = sbuf("ar2", [128, 32 * 1024], U8)
    x1f = ar2[:, 0:8192].bitcast(F32).rearrange("p (a d) -> p a d", a=2)
    onT = ar2[:, 0:32768].bitcast(BF16).rearrange("p (c t) -> p c t", c=8)
    gusb = [ar2[:, i * 16384:(i + 1) * 16384].bitcast(BF16).rearrange("p (c n) -> p c n", c=8)
            for i in range(2)]

    ar3 = sbuf("ar3", [128, 24 * 1024], U8)
    wqkv = ar3[:, 0:24576].bitcast(BF16).rearrange("p (c n) -> p c n", c=8)
    woT = ar3[:, 0:16384].bitcast(BF16).rearrange("p (c n) -> p c n", c=8)
    orw = ar3[0:65, 0:16384].bitcast(BF16).rearrange("p (h t) -> p h t", h=16)
    ostg = ar3[0:64, 16384:24576].bitcast(BF16).rearrange("p (h t) -> p h t", h=8)
    dnsb = [ar3[:, i * 8192:(i + 1) * 8192].bitcast(BF16)
            .rearrange("p (c n) -> p c n", c=4) for i in range(2)]

    PA = nc.alloc_psum_tensor("PA", [128, 4096], F32)
    banks = [Bank(PA[:, 512 * i:512 * (i + 1)]) for i in range(8)]

    s_xa = [SemC(nc, "sxa0"), SemC(nc, "sxa1")]
    s_sh = SemC(nc, "ssh"); s_st = SemC(nc, "sst")
    g_cst = SemC(nc, "gcst"); g_wq = SemC(nc, "gwq"); g_wo = SemC(nc, "gwo")
    g_buf = [SemC(nc, "gbuf0"), SemC(nc, "gbuf1")]
    g_dnb = [SemC(nc, "gdnb0"), SemC(nc, "gdnb1")]
    g_z = SemC(nc, "gz"); g_dbg = SemC(nc, "gdbg"); s_zb = SemC(nc, "szb")
    te = SemC(nc, "te"); ve = SemC(nc, "ve"); se = SemC(nc, "se"); gc = SemC(nc, "gc")
    DMAS = [s_xa[0], s_xa[1], s_sh, s_st, g_cst, g_wq, g_wo, g_buf[0], g_buf[1],
            g_dnb[0], g_dnb[1], g_z, g_dbg, s_zb]
    Y = EngQ("sync"); G = EngQ("gpsimd"); V = EngQ("vector"); C = EngQ("scalar"); T = EngQ("tensor")
    ALLQ = [Y, G, V, C, T]

    def vsep():
        V.do(lambda h: h.tensor_scalar_mul(junkv[:, 0:1], junkv[:, 1:2], 1.0))

    def dma(q, sem, out, in_):
        return q.do(lambda h, o=out, i=in_: h.dma_start(out=o, in_=i), inc=sem, n=16)

    def barrier():
        toks = [s.tok() for s in DMAS] + [te.tok(), ve.tok(), se.tok(), gc.tok()]
        for q in ALLQ:
            q.waits(toks)

    dma(G, g_cst, gsmall[:], gam_d)
    dma(G, g_cst, bsmall[:], bet_d)
    dma(G, g_cst, ebrow[:], ebias_d)
    t_gwT = dma(G, g_cst, gwT[:], gwT_d.rearrange("(c p) n -> p c n", p=128))
    t_gd1 = t_gwT

    iorow = sq_sc[:, 0:128]
    iocol = sq_sc[:, 128:256]
    G.do(lambda h: h.iota(iocol, [[1, 128]], channel_multiplier=0,
                          allow_small_or_imprecise_dtypes=True))
    G.do(lambda h: h.iota(iorow, [[0, 128]], channel_multiplier=1,
                          allow_small_or_imprecise_dtypes=True))
    G.do(lambda h: h.iota(selm[:], [[1, 16], [0, 64]], channel_multiplier=0,
                          allow_small_or_imprecise_dtypes=True))
    t_gc0 = G.do(lambda h: h.iota(selk[:], [[0, 1]], channel_multiplier=1,
                                  allow_small_or_imprecise_dtypes=True), inc=gc)
    V.wait(t_gc0)
    V.do(lambda h: h.tensor_tensor(ident[:], iocol, iorow, ALU.is_equal))
    V.do(lambda h: h.tensor_tensor(tril[:], iorow, iocol, ALU.is_le))
    V.do(lambda h: h.memset(ceps[:], EPS))
    V.do(lambda h: h.memset(lns[:], 0.0))
    V.do(lambda h: h.memset(junkv[:], 0.0))
    V.do(lambda h: h.memset(onesr[:], 1.0))
    V.do(lambda h: h.memset(
        qkpad.rearrange("p a (c g f) -> p a c g f", c=4, g=4)[:, :, :, :, 16:32], 0.0))
    t_sel = V.do(lambda h: h.tensor_scalar(selm[:], selm[:], selk[:, 0:1], None,
                                           ALU.is_equal), inc=ve)

    T.wait(t_gd1); T.wait(t_sel)
    b0, b1 = banks[0], banks[1]
    b0.pre_write(T)
    T.do(lambda h: h.matmul(b0.ap[0:128, 0:256], onesr[0:1, :],
                            gsmall[0:1, :].unsqueeze(1).broadcast_to([1, 16, FD]),
                            start=True, stop=False))
    t_mmg = T.do(lambda h: h.matmul(b0.ap[0:128, 256:512], onesr[0:1, :],
                                    bsmall[0:1, :].unsqueeze(1).broadcast_to([1, 16, FD]),
                                    start=False, stop=True), inc=te)
    b0.set_written(t_mmg)
    b1.pre_write(T)
    t_mme = T.do(lambda h: h.matmul(b1.ap[0:128, 0:16], onesr[0:1, :], ebrow[0:1, :],
                                    start=True, stop=True), inc=te)
    b1.set_written(t_mme)
    b0.pre_read(V)
    V.do(lambda h: h.tensor_scalar_mul(gamq[:], b0.ap[:, 0:256], 0.25))
    V.do(lambda h: h.tensor_copy(gamk[:], b0.ap[:, 0:256]))
    V.do(lambda h: h.tensor_scalar_mul(betq[:], b0.ap[:, 256:512], 0.25))
    t_vcb = V.do(lambda h: h.tensor_copy(betk[:], b0.ap[:, 256:512]), inc=ve)
    b0.add_reader(t_vcb)
    b1.pre_read(V)
    t_veb = V.do(lambda h: h.tensor_copy(ebias_t[:], b1.ap[:, 0:16]), inc=ve)
    b1.add_reader(t_veb)

    wtoks = [None, None]
    wtoks_dn = [None, None]
    def load_expert(eidx, buf, pretoks):
        G.waits(pretoks)
        t1 = dma(G, g_buf[buf], gusb[buf], gu_d[eidx].rearrange("(c p) n -> p c n", p=128))
        t2 = dma(G, g_dnb[buf], dnsb[buf], dn_d[eidx].rearrange("(c p) n -> p c n", p=128))
        wtoks[buf] = t1
        wtoks_dn[buf] = t2

    zbuf_last_tt = [[]]
    rzdr_bc = [[], []]
    attn_state = {"ostg_free": None, "o_c": None, "zt_free": None, "rz_free": None}

    def emit_zpath(P):
        V.waits(P["zrow"])
        V.wait(attn_state["rz_free"])
        V.do(lambda h: h.tensor_scalar_add(zt[:], zt[:], EPS))
        vsep()
        def _recip(h):
            with nc.allow_low_precision(reason="bf16 reciprocal of z (normalizer)"):
                return h.reciprocal(rz[:], zt[:])
        t_rz = V.do(_recip, inc=ve)
        attn_state["zt_free"] = t_rz
        par = P["par"]
        Y.waits(rzdr_bc[par])
        rzdr_bc[par][:] = []
        Y.wait(t_rz)
        t_zst = dma(Y, s_zb, rz_dr[par], rz[:])
        attn_state["rz_free"] = t_zst
        P["store"] = t_zst
        emit_bcast(P, 0)

    def emit_group(P, g):
        cc = P["cch"]
        V.wait(P["bc"][g])
        if g == 0:
            V.wait(attn_state["ostg_free"])
        last = []
        for p in (2 * g, 2 * g + 1):
            t_ne = V.do(lambda h, p=p, cc=cc: h.tensor_tensor(
                onT[0:64, p, 512 * cc:512 * (cc + 1)], orw[0:64, 2 * p, :],
                zbuf[:, (2 * p) % 4, :], ALU.mult), inc=ve)
            t_no = V.do(lambda h, p=p: h.tensor_tensor(
                ostg[0:64, p, :], orw[0:64, 2 * p + 1, :],
                zbuf[:, (2 * p + 1) % 4, :], ALU.mult), inc=ve)
            P["odd"].append(t_no)
            P["tt_last"] = t_no
            last = [t_ne, t_no]
        zbuf_last_tt[0] = last

    def emit_bcast(P, g):
        Y.waits(zbuf_last_tt[0])
        Y.wait(P["store"])
        tok = None
        for q in range(4):
            tok = dma(Y, s_zb, zbuf[:, q, :],
                      rz_dr[P["par"], 4 * g + q:4 * g + q + 1, :].to_broadcast((64, 512)))
        P["bc"][g] = tok
        rzdr_bc[P["par"]].append(tok)

    def finish_chunk(P):
        cc = P["cch"]
        Y.waits(P["odd"])
        t_sh = dma(Y, s_sh, onT[64:128, :, 512 * cc:512 * (cc + 1)], ostg[0:64, :, :])
        attn_state["ostg_free"] = t_sh
        attn_state["o_c"].append((P["tt_last"], t_sh))

    prev_store_tok = None
    for s in range(SEQN):
        barrier()
        t_wqkv = dma(G, g_wq, wqkv, wqkvT_d.rearrange("(c p) n -> p c n", p=128))
        t_ones = V.do(lambda h: h.memset(vsb[:, :, :, 64:65], 1.0), inc=ve)

        xa_tok = [None, None]
        xa_free = [None, None]
        x1f_free = [None, None]
        x1t_free = None
        qkpad_free = None
        fmsc_free = None
        fmv_free = None
        vsb_toks = []
        t_qev = None

        def p1_head(tt):
            par = tt % 2
            if tt < 2:
                xa_tok[par] = dma(Y, s_xa[par], xa[:, par, :], x_in[s, tt * 128:(tt + 1) * 128, :])
            V.wait(xa_tok[par]); C.wait(xa_tok[par])
            V.do(lambda h, p=par: h.tensor_reduce(lns[:, 0:1], xa[:, p, :], AX.X, ALU.add))
            C.wait(fmv_free)
            t_sq = C.do(lambda h, p=par: h.activation(sq_sc[:], xa[:, p, :], AF.Square,
                                                      accum_out=lns[:, 1:2]), inc=se)
            V.wait(t_sq)
            vsep()
            V.do(lambda h: h.tensor_scalar_mul(lns[:, 2:3], lns[:, 0:1], 1.0 / D))
            vsep()
            V.do(lambda h: h.tensor_tensor(lns[:, 3:4], lns[:, 2:3], lns[:, 2:3], ALU.mult))
            vsep()
            t_var = V.do(lambda h: h.scalar_tensor_tensor(lns[:, 4:5], lns[:, 1:2], 1.0 / D,
                                                          lns[:, 3:4], ALU.mult, ALU.subtract), inc=ve)
            C.wait(t_var)
            t_sdt = C.do(lambda h: h.activation(lns[:, 5:6], lns[:, 4:5], AF.Sqrt,
                                                bias=ceps[:, 0:1]), inc=se)
            V.wait(t_sdt)
            V.do(lambda h: h.reciprocal(lns[:, 6:7], lns[:, 5:6]))
            vsep()
            V.wait(x1f_free[par])
            t_x1 = V.do(lambda h, p=par: h.tensor_scalar(x1f[:, p, :], xa[:, p, :], lns[:, 2:3],
                                                         lns[:, 6:7], ALU.subtract, ALU.mult), inc=ve)
            xa_free[par] = t_x1
            if tt + 2 < NT:
                Y.wait(xa_free[par]); Y.wait(t_sq)
                xa_tok[par] = dma(Y, s_xa[par], xa[:, par, :],
                                  x_in[s, (tt + 2) * 128:(tt + 3) * 128, :])
            return t_x1

        p1_state = {}
        for _t1 in range(NT + 1):
            if _t1 < NT:
                p1_state[_t1] = p1_head(_t1)
            if _t1 == 0:
                continue
            tt = _t1 - 1
            par = tt % 2
            t_x1 = p1_state.pop(tt)
            T.wait(t_x1)
            T.wait(x1t_free)
            t_x1ev = None
            t_tr = None
            for half in range(2):
                tb = banks[3] if half == 0 else banks[4]
                tb.pre_write(T)
                for c4 in range(4):
                    c = half * 4 + c4
                    t_tr = T.do(lambda h, c=c, c4=c4, tb=tb, p=par: h.transpose(
                        tb.ap[:, c4 * 128:(c4 + 1) * 128], x1f[:, p, c * 128:(c + 1) * 128],
                        ident[:]), inc=te)
                tb.set_written(t_tr)
                tb.pre_read(V)
                V.wait(x1t_free)
                t_x1ev = V.do(lambda h, half=half, tb=tb: h.tensor_copy(
                    x1t[:, half * 4:half * 4 + 4, :].rearrange("p c t -> p (c t)"),
                    tb.ap[:]), inc=ve)
                tb.add_reader(t_x1ev)
            x1f_free[par] = t_tr
            t_x1tr_te = t_tr
            qkb, vb0, vb1 = ((banks[0], banks[1], banks[2]) if par == 0
                             else (banks[5], banks[6], banks[7]))
            T.wait(t_x1ev); T.wait(t_wqkv)
            for bnk in (qkb, vb0, vb1):
                bnk.pre_write(T)
            t_mm = None
            for c in range(8):
                for ni, bnk in ((0, qkb), (1, vb0), (2, vb1)):
                    t_mm = T.do(lambda h, c=c, ni=ni, bnk=bnk: h.matmul(
                        bnk.ap[:], x1t[:, c, :], wqkv[:, c, 512 * ni:512 * (ni + 1)],
                        start=(c == 0), stop=(c == 7)),
                        inc=(te if (c == 7 and ni == 2) else None))
            for bnk in (qkb, vb0, vb1):
                bnk.set_written(t_mm)
            x1t_free = t_mm
            V.wait(t_ones)
            vb0.pre_read(V); vb1.pre_read(V)
            V.do(lambda h, tt=tt, vb0=vb0: h.tensor_copy(
                vsb[:, tt, 0:8, 0:64], vb0.ap[:].rearrange("p (h c) -> p h c", h=8)))
            t_vev = V.do(lambda h, tt=tt, vb1=vb1: h.tensor_copy(
                vsb[:, tt, 8:16, 0:64], vb1.ap[:].rearrange("p (h c) -> p h c", h=8)), inc=ve)
            vb0.add_reader(t_vev); vb1.add_reader(t_vev)
            vsb_toks.append(t_vev)
            C.wait(t_x1tr_te)
            fm_toks = []
            t_t1 = None
            t_fsq = None
            for qi in range(2):
                g_t = gamq if qi == 0 else gamk
                b_t = betq if qi == 0 else betk
                off = 256 * qi
                qkb.pre_read(C); qkb.pre_read(V)
                t_fsq = C.do(lambda h, off=off, qkb=qkb: h.activation(
                    sq_sc[:, 0:256], qkb.ap[:, off:off + 256], AF.Square), inc=se)
                V.wait(t_fsq)
                V.do(lambda h, off=off, qkb=qkb: h.tensor_reduce(
                    fms[:, 0, :], qkb.ap[:, off:off + 256].rearrange("p (a b) -> p a b", a=16),
                    AX.X, ALU.add))
                V.do(lambda h: h.tensor_reduce(
                    fms[:, 1, :], sq_sc[:, 0:256].rearrange("p (a b) -> p a b", a=16),
                    AX.X, ALU.add))
                V.do(lambda h: h.tensor_scalar_mul(fms[:, 2, :], fms[:, 0, :], 1.0 / FD))
                vsep()
                V.do(lambda h: h.tensor_tensor(fms[:, 3, :], fms[:, 2, :], fms[:, 2, :], ALU.mult))
                vsep()
                t_fv = V.do(lambda h: h.scalar_tensor_tensor(fms[:, 4, :], fms[:, 1, :], 1.0 / FD,
                                                             fms[:, 3, :], ALU.mult, ALU.subtract),
                            inc=ve)
                fmv_free = t_fv
                C.wait(t_fv)
                t_fsd = C.do(lambda h: h.activation(fms[:, 5, :], fms[:, 4, :], AF.Sqrt,
                                                    bias=ceps[:, 0:1]), inc=se)
                V.wait(t_fsd)
                V.do(lambda h: h.reciprocal(fms[:, 6, :], fms[:, 5, :]))
                V.wait(fmsc_free)
                vsep()
                V.do(lambda h, g_t=g_t: h.tensor_tensor(
                    fmsc[:, 0, :].rearrange("p (a b) -> p a b", a=16),
                    fms[:, 6, :].unsqueeze(2).broadcast_to([128, 16, FD]),
                    g_t[:].rearrange("p (a b) -> p a b", a=16), ALU.mult))
                t_t1 = V.do(lambda h, off=off, qkb=qkb: h.tensor_tensor(
                    fmsc[:, 1, :].rearrange("p (a b) -> p a b", a=16),
                    qkb.ap[:, off:off + 256].rearrange("p (a b) -> p a b", a=16),
                    fms[:, 2, :].unsqueeze(2).broadcast_to([128, 16, FD]),
                    ALU.subtract), inc=ve)
                G.wait(t_t1)
                G.wait(qkpad_free)
                G.do(lambda h: h.tensor_tensor(
                    fmsc[:, 1, :], fmsc[:, 1, :], fmsc[:, 0, :], ALU.mult))
                t_pad = G.do(lambda h, qi=qi, b_t=b_t: h.tensor_tensor(
                    qkpad[:, qi, :].rearrange("p (c g f) -> p c g f", c=4, g=4)[:, :, :, 0:16],
                    fmsc[:, 1, :].rearrange("p (c g b) -> p c g b", c=4, g=4),
                    b_t[:].rearrange("p (c g b) -> p c g b", c=4, g=4), ALU.add), inc=gc)
                fm_toks.append(t_pad)
                fmsc_free = t_pad
            fmsc_free = fm_toks[-1]
            qkb.add_reader(t_t1); qkb.add_reader(t_fsq)
            T.waits(fm_toks)
            t_tr2 = None
            for qi in range(2):
                dstT = qT if qi == 0 else kT
                tb = banks[3] if qi == 0 else banks[4]
                tb.pre_write(T)
                for ci in range(4):
                    t_tr2 = T.do(lambda h, qi=qi, ci=ci, tb=tb: h.transpose(
                        tb.ap[:, ci * 128:(ci + 1) * 128],
                        qkpad[:, qi, ci * 128:(ci + 1) * 128], ident[:]), inc=te)
                tb.set_written(t_tr2)
                tb.pre_read(V)
                t_qev = V.do(lambda h, tt=tt, dstT=dstT, tb=tb: h.tensor_copy(
                    dstT[:, :, tt * 128:(tt + 1) * 128],
                    tb.ap[:].rearrange("p (c t) -> p c t", c=4)), inc=ve)
                tb.add_reader(t_qev)
            qkpad_free = t_tr2

        t_pass1_v = t_qev

        if debug >= 1 and s == 0:
            G.wait(t_pass1_v); G.waits(vsb_toks)
            dma(G, g_dbg, dbg["x1"], sq_sc[:])
            dma(G, g_dbg, dbg["lns"], lns[:])
            dma(G, g_dbg, dbg["x1t"], x1t[:].rearrange("p c t -> p (c t)"))
            dma(G, g_dbg, dbg["qT"], qT.rearrange("p c t -> p (c t)"))
            dma(G, g_dbg, dbg["kT"], kT.rearrange("p c t -> p (c t)"))
            dma(G, g_dbg, dbg["v"], vsb.rearrange("p t h c -> p (t h c)"))

        if stage < 2:
            continue
        o_c_toks = []
        st2_rd = [None, None]
        zd_prev = [None, None, None, None]
        pending = None
        attn_state["o_c"] = o_c_toks
        for cch in range(NCH):
            njt = 4 * cch + 4
            zrow_dma_toks = []
            for w in range(4):
                svb = [banks[4 + q] for q in range(4)]
                qkbk = [banks[q] for q in range(4)]
                t_sv = None
                pend = None
                def emit_sv(jinfo, is_first, is_last):
                    j, jp, st_col, toks4 = jinfo
                    T.waits(toks4)
                    T.wait(vsb_toks[j])
                    t_sv_ = None
                    for q in range(4):
                        if is_first:
                            svb[q].pre_write(T)
                        hh = 4 * w + q
                        t_sv_ = T.do(lambda h, q=q, hh=hh, j=j, jp=jp, st=st_col, svb=svb,
                                     st_=is_first, sp_=is_last: h.matmul(
                            svb[q].ap[0:65, st:512], vsb[:, j, hh, 0:65], st2[:, jp, q, st:512],
                            start=st_, stop=sp_),
                            inc=(te if q == 3 else None))
                    return t_sv_
                for j in range(njt):
                    jp = j % 2
                    diag = j >= 4 * cch
                    st_col = 128 * (j - 4 * cch) if diag else 0
                    T.wait(t_pass1_v)
                    for q in range(4):
                        qkbk[q].pre_write(T)
                    t_qk = None
                    for q in range(4):
                        for m in range(4):
                            t_qk = T.do(lambda h, q=q, m=m, w=w, j=j, st=st_col, cch=cch: h.matmul(
                                qkbk[q].ap[32 * m:32 * m + 32, st:512],
                                kT[32 * q:32 * q + 16, w, j * 128 + 32 * m: j * 128 + 32 * m + 32],
                                qT[32 * q:32 * q + 16, w, 512 * cch + st: 512 * (cch + 1)],
                                tile_position=(32 * q, 32 * m)),
                                inc=(te if (q == 3 and m == 3) else None))
                    for q in range(4):
                        qkbk[q].set_written(t_qk)
                    toks4 = []
                    for q in (0, 1, 2):
                        qkbk[q].pre_read(C)
                        C.wait(st2_rd[jp])
                        t_e = C.do(lambda h, q=q, jp=jp, st=st_col, b=qkbk[q]: h.activation(
                            st2[:, jp, q, st:512], b.ap[:, st:512], AF.Square), inc=se)
                        qkbk[q].add_reader(t_e)
                        toks4.append(t_e)
                    qkbk[3].pre_read(V)
                    V.wait(st2_rd[jp])
                    t_cp = V.do(lambda h, jp=jp, st=st_col, b=qkbk[3]: h.tensor_copy(
                        st2[:, jp, 3, st:512], b.ap[:, st:512]), inc=ve)
                    qkbk[3].add_reader(t_cp)
                    G.wait(t_cp)
                    mt = G.do(lambda h, jp=jp, st=st_col: h.tensor_tensor(
                        st2[:, jp, 3, st:512], st2[:, jp, 3, st:512],
                        st2[:, jp, 3, st:512], ALU.mult), inc=(gc if not diag else None))
                    if diag:
                        V.waits(toks4)
                        t_mv = None
                        for q in (0, 1, 2):
                            t_mv = V.do(lambda h, q=q, jp=jp, st=st_col: h.tensor_tensor(
                                st2[:, jp, q, st:st + 128], st2[:, jp, q, st:st + 128],
                                tril[:], ALU.mult), inc=(ve if q == 2 else None))
                        mt = G.do(lambda h, jp=jp, st=st_col: h.tensor_tensor(
                            st2[:, jp, 3, st:st + 128], st2[:, jp, 3, st:st + 128],
                            tril[:], ALU.mult), inc=gc)
                        toks4 = [t_mv, mt]
                    else:
                        toks4.append(mt)
                    if pend is not None:
                        t_sv = emit_sv(pend, pend[0] == 0, False)
                        st2_rd[pend[1]] = t_sv
                    pend = (j, jp, st_col, toks4)
                    if pending is not None and w == 0 and j == 3:
                        emit_zpath(pending)
                t_sv = emit_sv(pend, pend[0] == 0, True)
                st2_rd[pend[1]] = t_sv
                pend = None
                for q in range(4):
                    svb[q].set_written(t_sv)
                if pending is not None:
                    emit_group(pending, w)
                    if w <= 2:
                        emit_bcast(pending, w + 1)
                    if w == 3:
                        finish_chunk(pending)
                for q in range(4):
                    svb[q].pre_read(V)
                V.wait(zd_prev[w])
                t_or = V.do(lambda h, w=w: h.tensor_copy(
                    orw[0:65, 4 * w:4 * w + 4, :],
                    PA[0:65, 2048:4096].rearrange("p (h t) -> p h t", h=4)), inc=ve)
                for q in range(4):
                    svb[q].add_reader(t_or)
                G.wait(t_or)
                G.wait(attn_state["zt_free"])
                t_zd = dma(G, g_z, zt[4 * w:4 * w + 4, :], orw[64:65, 4 * w:4 * w + 4, :])
                zrow_dma_toks.append(t_zd)
            zd_prev = zrow_dma_toks
            pending = {"cch": cch, "par": cch % 2, "zrow": zrow_dma_toks,
                       "store": None, "bc": {}, "tt_last": None, "odd": []}
        emit_zpath(pending)
        emit_group(pending, 0)
        emit_bcast(pending, 1)
        emit_group(pending, 1)
        emit_bcast(pending, 2)
        emit_group(pending, 2)
        emit_bcast(pending, 3)
        emit_group(pending, 3)
        finish_chunk(pending)
        pending = None

        if debug >= 2 and s == 0:
            for pt in o_c_toks:
                G.waits(list(pt))
            dma(G, g_dbg, dbg["onT"], onT.rearrange("p c t -> p (c t)"))

        for pt in o_c_toks:
            G.waits(list(pt))
        t_woT = dma(G, g_wo, woT, woT_d.rearrange("(c p) n -> p c n", p=128))

        if stage < 3:
            continue
        t_sh_all = s_sh.tok()
        xr_tok = [None, None]
        xr_free = [None, None]
        x2t_toks = []
        gts_free = None
        t_wt = None
        sq2_free = None
        t_wo = None
        def p2_head(tt):
            par = tt % 2
            if tt < 2:
                Y.wait(fmsc_free)
                Y.wait(qkpad_free)
                xr_tok[par] = dma(Y, s_xa[par], xr[:, par, :], x_in[s, tt * 128:(tt + 1) * 128, :])
            cch = tt // 4
            wob = (banks[2], banks[3]) if par == 0 else (banks[4], banks[5])
            T.wait(o_c_toks[cch][0])
            T.wait(t_sh_all)
            T.wait(t_woT)
            for bnk in wob:
                bnk.pre_write(T)
            t_wo = None
            for c in range(8):
                for ni in range(2):
                    t_wo = T.do(lambda h, c=c, ni=ni, tt=tt, wob=wob: h.matmul(
                        wob[ni].ap[:], onT[:, c, tt * 128:(tt + 1) * 128],
                        woT[:, c, 512 * ni:512 * (ni + 1)],
                        start=(c == 0), stop=(c == 7)),
                        inc=(te if (c == 7 and ni == 1) else None))
            for bnk in wob:
                bnk.set_written(t_wo)
            V.wait(xr_tok[par])
            wob[0].pre_read(V); wob[1].pre_read(V)
            V.do(lambda h, tt=tt, p=par, wob=wob: h.tensor_tensor(
                ysb[:, tt, 0:512], wob[0].ap[:], xr[:, p, 0:512], ALU.add))
            t_res = V.do(lambda h, tt=tt, p=par, wob=wob: h.tensor_tensor(
                ysb[:, tt, 512:1024], wob[1].ap[:], xr[:, p, 512:1024], ALU.add), inc=ve)
            wob[0].add_reader(t_res); wob[1].add_reader(t_res)
            xr_free[par] = t_res
            if tt + 2 < NT:
                Y.wait(xr_free[par])
                xr_tok[par] = dma(Y, s_xa[par], xr[:, par, :], x_in[s, (tt + 2) * 128:(tt + 3) * 128, :])
            return t_res, t_wo

        p2_state = {}
        for tt in range(NT + 1):
            if tt < NT:
                p2_state[tt] = p2_head(tt)
            if tt == 0:
                continue
            tt = tt - 1
            par = tt % 2
            t_res, t_wo = p2_state.pop(tt)
            if tt == NT - 1:
                t_wo_last_v = t_wo
            C.wait(t_res)
            C.wait(sq2_free)
            C.wait(gts_free)
            vsep()
            V.do(lambda h, tt=tt: h.tensor_reduce(lns[:, 0:1], ysb[:, tt, :], AX.X, ALU.add))
            t_sq2 = C.do(lambda h, tt=tt: h.activation(sq_sc[:], ysb[:, tt, :], AF.Square,
                                                       accum_out=lns[:, 1:2]), inc=se)
            V.wait(t_sq2)
            vsep()
            V.do(lambda h: h.tensor_scalar_mul(lns[:, 2:3], lns[:, 0:1], 1.0 / D))
            vsep()
            V.do(lambda h: h.tensor_tensor(lns[:, 3:4], lns[:, 2:3], lns[:, 2:3], ALU.mult))
            vsep()
            t_var2 = V.do(lambda h: h.scalar_tensor_tensor(lns[:, 4:5], lns[:, 1:2], 1.0 / D,
                                                           lns[:, 3:4], ALU.mult, ALU.subtract), inc=ve)
            C.wait(t_var2)
            t_sd2 = C.do(lambda h: h.activation(lns[:, 5:6], lns[:, 4:5], AF.Sqrt,
                                                bias=ceps[:, 0:1]), inc=se)
            V.wait(t_sd2)
            V.do(lambda h: h.reciprocal(lns[:, 6:7], lns[:, 5:6]))
            vsep()
            t_x2 = V.do(lambda h, tt=tt: h.tensor_scalar(sq_sc[:], ysb[:, tt, :], lns[:, 2:3],
                                                         lns[:, 6:7], ALU.subtract, ALU.mult), inc=ve)
            T.wait(t_x2)
            t_x2tv = None
            t_tr3 = None
            for half in range(2):
                tb = banks[6] if half == 0 else banks[7]
                tb.pre_write(T)
                for c4 in range(4):
                    c = half * 4 + c4
                    t_tr3 = T.do(lambda h, c=c, c4=c4, tb=tb: h.transpose(
                        tb.ap[:, c4 * 128:(c4 + 1) * 128], sq_sc[:, c * 128:(c + 1) * 128],
                        ident[:]), inc=te)
                tb.set_written(t_tr3)
                tb.pre_read(V)
                t_x2tv = V.do(lambda h, tt=tt, half=half, tb=tb: h.tensor_copy(
                    x2t[:, half * 4:half * 4 + 4, tt * 128:(tt + 1) * 128],
                    tb.ap[:].rearrange("p (c t) -> p c t", c=4)), inc=ve)
                tb.add_reader(t_x2tv)
            sq2_free = t_tr3
            x2t_toks.append(t_x2tv)
            lgb = banks[0] if par == 0 else banks[1]
            T.wait(t_x2tv); T.wait(t_gwT)
            lgb.pre_write(T)
            t_lg = None
            for c in range(8):
                t_lg = T.do(lambda h, c=c, tt=tt, lgb=lgb: h.matmul(
                    lgb.ap[:, 0:16], x2t[:, c, tt * 128:(tt + 1) * 128], gwT[:, c, :],
                    start=(c == 0), stop=(c == 7)), inc=(te if c == 7 else None))
            lgb.set_written(t_lg)
            lgb.pre_read(C)
            C.wait(gts_free)
            t_sc = C.do(lambda h, lgb=lgb: h.activation(gts[:, 0:16], lgb.ap[:, 0:16],
                                                        AF.Sigmoid), inc=se)
            lgb.add_reader(t_sc)
            V.wait(t_sc); V.wait(t_veb)
            g = gts
            V.do(lambda h: h.tensor_tensor(g[:, 16:32], g[:, 0:16], ebias_t[:], ALU.add))
            pr = [(0, 1), (0, 2), (0, 3), (1, 2), (1, 3), (2, 3)]
            vsep()
            for i, (a, b_) in enumerate(pr):
                V.do(lambda h, a=a, b_=b_, i=i: h.tensor_tensor(
                    g[:, 32 + 4 * i:36 + 4 * i].unsqueeze(2),
                    g[:, 16:32].rearrange("p (x y) -> p x y", y=4)[:, :, a:a + 1],
                    g[:, 16:32].rearrange("p (x y) -> p x y", y=4)[:, :, b_:b_ + 1], ALU.add))
            V.do(lambda h: h.tensor_tensor(g[:, 56:60], g[:, 32:36], g[:, 36:40], ALU.max))
            V.do(lambda h: h.tensor_tensor(g[:, 60:64], g[:, 40:44], g[:, 44:48], ALU.max))
            V.do(lambda h: h.tensor_tensor(g[:, 64:68], g[:, 48:52], g[:, 52:56], ALU.max))
            V.do(lambda h: h.tensor_tensor(g[:, 56:60], g[:, 56:60], g[:, 60:64], ALU.max))
            vsep()
            V.do(lambda h: h.tensor_tensor(g[:, 56:60], g[:, 56:60], g[:, 64:68], ALU.max))
            vsep()
            V.do(lambda h: h.tensor_reduce(lns[:, 0:1], g[:, 56:60], AX.X, ALU.max))
            vsep()
            V.do(lambda h: h.tensor_scalar(g[:, 60:64], g[:, 56:60], lns[:, 0:1], None, ALU.is_ge))
            vsep()
            V.do(lambda h: h.scalar_tensor_tensor(g[:, 60:64], g[:, 60:64], -1e30,
                                                  g[:, 56:60], ALU.mult, ALU.add))
            vsep()
            V.do(lambda h: h.tensor_reduce(lns[:, 1:2], g[:, 60:64], AX.X, ALU.max))
            vsep()
            V.do(lambda h: h.tensor_scalar(g[:, 60:64], g[:, 56:60], lns[:, 1:2], None, ALU.is_ge))
            vsep()
            V.do(lambda h: h.tensor_tensor(
                g[:, 32:48].rearrange("p (x y) -> p x y", y=4),
                g[:, 16:32].rearrange("p (x y) -> p x y", y=4),
                g[:, 60:64].unsqueeze(2).broadcast_to([128, 4, 4]), ALU.mult))
            vsep()
            V.do(lambda h: h.tensor_copy(g[:, 64:80], g[:, 32:48]))
            vsep()
            for r in range(3):
                V.do(lambda h: h.tensor_reduce(lns[:, 0:1], g[:, 64:80], AX.X, ALU.max))
                vsep()
                V.do(lambda h: h.tensor_scalar(g[:, 48:64], g[:, 64:80], lns[:, 0:1], None,
                                               ALU.is_ge))
                vsep()
                V.do(lambda h: h.scalar_tensor_tensor(g[:, 64:80], g[:, 48:64], -1e30,
                                                      g[:, 64:80], ALU.mult, ALU.add))
                vsep()
            V.do(lambda h: h.tensor_reduce(lns[:, 0:1], g[:, 64:80], AX.X, ALU.max))
            vsep()
            V.do(lambda h: h.tensor_scalar(g[:, 48:64], g[:, 32:48], lns[:, 0:1], None, ALU.is_ge))
            vsep()
            V.do(lambda h: h.tensor_tensor(g[:, 64:80], g[:, 0:16], g[:, 48:64], ALU.mult))
            vsep()
            V.do(lambda h: h.tensor_reduce(lns[:, 0:1], g[:, 64:80], AX.X, ALU.add))
            vsep()
            V.do(lambda h: h.tensor_scalar_add(lns[:, 0:1], lns[:, 0:1], 1e-20))
            vsep()
            V.do(lambda h: h.reciprocal(lns[:, 1:2], lns[:, 0:1]))
            vsep()
            t_W = V.do(lambda h: h.tensor_scalar(g[:, 48:64], g[:, 64:80], lns[:, 1:2], 2.5,
                                                 ALU.mult, ALU.mult), inc=ve)
            gts_free = t_W
            tbW = banks[6]
            T.wait(t_W)
            tbW.pre_write(T)
            t_trW = T.do(lambda h, tbW=tbW: h.transpose(tbW.ap[0:16, 0:128], g[:, 48:64],
                                                        ident[:]), inc=te)
            tbW.set_written(t_trW)
            tbW.pre_read(V)
            t_wt = V.do(lambda h, tt=tt, tbW=tbW: h.tensor_copy(
                wt[0:16, tt * 128:(tt + 1) * 128], tbW.ap[0:16, 0:128]), inc=ve)
            tbW.add_reader(t_wt)

        t_wo_last = t_wo_last_v
        if debug >= 3 and s == 0:
            G.wait(t_wt)
            G.waits(x2t_toks)
            dma(G, g_dbg, dbg["x2"], ysb.rearrange("p t d -> p (t d)"))
            dma(G, g_dbg, dbg["x2t"], x2t[:].rearrange("p c t -> p (c t)"))
            t_dbg3 = dma(G, g_dbg, dbg["W"], wt[:])
            V.wait(t_dbg3)
            T.wait(t_dbg3)

        if stage < 4:
            continue
        t_x2t_all = x2t_toks[-1]
        t_wt_all = t_wt
        groups = [[0, 1], [2, 3], [4, 5], [6, 7], [8, 9], [10, 11],
                  [12, 13], [14, 15], [16]]
        NOWEB = 0
        PHB = 3
        eorder = [e for grp in groups for e in grp]
        load_expert(eorder[0], 0, [t_wo_last])
        load_expert(eorder[1], 1, [t_wo_last])
        yev_tok = None
        intT_free = [None, None]
        web_free = None
        sg_free = None
        t1_free = None
        store_toks = []
        exp_final_tok = [None] * len(eorder)
        eidx_flat = 0
        for gi, grp in enumerate(groups):
            for t256 in range(8):
                ybanks = [banks[i] for i in range(4)]
                for bnk in ybanks:
                    bnk.pre_write(T)
                t_dn = None
                for ei, e in enumerate(grp):
                    flat = eidx_flat + ei
                    buf = flat % 2
                    is_shared = (e == 16) or NOWEB
                    T.wait(wtoks[buf])
                    if PHB == 0:
                        t_dn = None
                        continue
                    t_wev = None
                    if not is_shared:
                        wb = banks[6]
                        wb.pre_write(T)
                        T.wait(t_wt_all)
                        T.do(lambda h, e=e, t256=t256, wb=wb: h.matmul(
                            wb.ap[0:64, 0:256], selm[0:16, e * 64:(e + 1) * 64],
                            wt[0:16, 256 * t256:256 * (t256 + 1)],
                            start=True, stop=True))
                        t_web = T.do(lambda h, e=e, t256=t256, wb=wb: h.matmul(
                            wb.ap[64:128, 0:256], selm[0:16, e * 64:(e + 1) * 64],
                            wt[0:16, 256 * t256:256 * (t256 + 1)],
                            start=True, stop=True, tile_position=(0, 64)), inc=te)
                        wb.set_written(t_web)
                        wb.pre_read(C)
                        C.wait(web_free)
                        t_wev = C.do(lambda h, wb=wb: h.activation(
                            websb[:], wb.ap[:, 0:256], AF.Identity), inc=se)
                        wb.add_reader(t_wev)
                    def emit_gu(ic):
                        gub = banks[4 + (ic % 2)]
                        gub.pre_write(T)
                        T.wait(t_x2t_all)
                        t_gu = None
                        for c in range(8):
                            for half in range(2):
                                t_gu = T.do(lambda h, c=c, ic=ic, half=half, t256=t256,
                                            buf=buf, gub=gub: h.matmul(
                                    gub.ap[:, 256 * half:256 * half + 256],
                                    gusb[buf][:, c, 512 * half + 128 * ic: 512 * half + 128 * ic + 128],
                                    x2t[:, c, 256 * t256:256 * (t256 + 1)],
                                    start=(c == 0 and half == 0), stop=(c == 7 and half == 1)),
                                    inc=(te if (c == 7 and half == 1) else None))
                        gub.set_written(t_gu)
                        return gub
                    gubs = {0: emit_gu(0), 1: emit_gu(1)}
                    t_dn = None
                    for ic in range(4):
                        gub = gubs.pop(ic)
                        if PHB == 1:
                            if ic + 2 < 4:
                                gubs[ic + 2] = emit_gu(ic + 2)
                            continue
                        gub.pre_read(C)
                        C.wait(sg_free)
                        t_sg = C.do(lambda h, gub=gub: h.activation(
                            sgsc[:], gub.ap[:, 0:256], AF.Silu), inc=se)
                        gub.pre_read(V)
                        V.wait(t_sg)
                        V.wait(t1_free)
                        if not is_shared:
                            V.wait(t_wev)
                            t_t1m = V.do(lambda h, gub=gub: h.tensor_tensor(
                                t1sc[:], gub.ap[:, 256:512], websb[:], ALU.mult), inc=ve)
                        else:
                            t_t1m = V.do(lambda h, gub=gub: h.tensor_copy(
                                t1sc[:], gub.ap[:, 256:512]), inc=ve)
                        gub.add_reader(t_sg); gub.add_reader(t_t1m)
                        V.wait(intT_free[ic % 2])
                        vsep()
                        t_int = V.do(lambda h, ic=ic: h.tensor_tensor(
                            intT[:, ic % 2, :], t1sc[:], sgsc[:], ALU.mult), inc=ve)
                        sg_free = t_int
                        t1_free = t_int
                        if not is_shared:
                            web_free = t_int
                        if ic + 2 < 4:
                            gubs[ic + 2] = emit_gu(ic + 2)
                        if PHB == 2:
                            continue
                        T.wait(t_int)
                        T.wait(wtoks_dn[buf])
                        first = (ei == 0 and ic == 0)
                        last = (ei == len(grp) - 1 and ic == 3)
                        for tth in range(2):
                            T.do(lambda h, ic=ic, tth=tth, buf=buf, yb=ybanks[2 * tth],
                                 first=first, last=last: h.matmul(
                                yb.ap[:], intT[:, ic % 2, 128 * tth:128 * tth + 128],
                                dnsb[buf][:, ic, 0:512],
                                start=first, stop=last))
                            t_dn = T.do(lambda h, ic=ic, tth=tth, buf=buf, yb=ybanks[2 * tth + 1],
                                        first=first, last=last: h.matmul(
                                yb.ap[:], intT[:, ic % 2, 128 * tth:128 * tth + 128],
                                dnsb[buf][:, ic, 512:1024],
                                start=first, stop=last),
                                inc=(te if tth == 1 else None))
                        intT_free[ic % 2] = t_dn
                    if t256 == 7:
                        exp_final_tok[flat] = t_dn
                for bnk in ybanks:
                    bnk.set_written(t_dn)
                if PHB < 3:
                    continue
                for tth in range(2):
                    tt = 2 * t256 + tth
                    yb0, yb1 = ybanks[2 * tth], ybanks[2 * tth + 1]
                    yb0.pre_read(V); yb1.pre_read(V)
                    V.do(lambda h, tt=tt, yb=yb0: h.tensor_tensor(
                        ysb[:, tt, 0:512], yb.ap[:], ysb[:, tt, 0:512], ALU.add))
                    yev_tok = V.do(lambda h, tt=tt, yb=yb1: h.tensor_tensor(
                        ysb[:, tt, 512:1024], yb.ap[:], ysb[:, tt, 512:1024], ALU.add), inc=ve)
                    yb0.add_reader(yev_tok); yb1.add_reader(yev_tok)
                    if gi == len(groups) - 1:
                        Y.wait(yev_tok)
                        store_toks.append(dma(Y, s_st, y_out[s, tt * 128:(tt + 1) * 128, :],
                                              ysb[:, tt, :]))
            for ei in range(len(grp)):
                flat = eidx_flat + ei
                nxt = flat + 2
                if nxt < len(eorder):
                    load_expert(eorder[nxt], nxt % 2, [exp_final_tok[flat]])
            eidx_flat += len(grp)

        if not store_toks:
            for tt in range(NT):
                Y.wait(yev_tok or te.tok())
                store_toks.append(dma(Y, s_st, y_out[s, tt * 128:(tt + 1) * 128, :],
                                      ysb[:, tt, :]))
        prev_store_tok = store_toks[-1] if store_toks else None

    for sset in DMAS:
        Y.wait(sset.tok())

    with nc.Block() as block:
        @block.sync
        def _(h):
            for f in Y.ops:
                f(h)
        @block.gpsimd
        def _(h):
            for f in G.ops:
                f(h)
        @block.vector
        def _(h):
            for f in V.ops:
                f(h)
        @block.scalar
        def _(h):
            for f in C.ops:
                f(h)
        @block.tensor
        def _(h):
            for f in T.ops:
                f(h)

    return nc


def make_inputs(ins, core):
    x = np.ascontiguousarray(np.asarray(ins["L0_attn_x"])[2 * core:2 * core + 2])
    wqkvT = np.ascontiguousarray(
        np.concatenate([np.asarray(ins["L0_attn_Wq"]).T, np.asarray(ins["L0_attn_Wk"]).T,
                        np.asarray(ins["L0_attn_Wv"]).T], axis=1))
    woT = np.ascontiguousarray(np.asarray(ins["L0_attn_Wo"]).T)
    gp = np.asarray(ins["L0_ffn_gate_proj"])
    up = np.asarray(ins["L0_ffn_up_proj"])
    dp = np.asarray(ins["L0_ffn_down_proj"])
    gu = np.concatenate([np.transpose(gp, (0, 2, 1)), np.transpose(up, (0, 2, 1))], axis=2)
    shgu = np.concatenate([np.asarray(ins["L0_ffn_shared_gate_w"]).T,
                           np.asarray(ins["L0_ffn_shared_up_w"]).T], axis=1)[None]
    gu = np.concatenate([gu, shgu], axis=0)
    dn = np.transpose(dp, (0, 2, 1))
    dn = np.concatenate([dn, np.asarray(ins["L0_ffn_shared_down_w"]).T[None]], axis=0)
    return {
        "x": x.astype(np.float32),
        "wqkvT": wqkvT.astype(np.float32),
        "woT": woT.astype(np.float32),
        "gu": np.ascontiguousarray(gu, dtype=np.float32),
        "dn": np.ascontiguousarray(dn, dtype=np.float32),
        "gwT_in": np.ascontiguousarray(np.asarray(ins["L0_ffn_gate_w"]).T, dtype=np.float32),
        "gam": np.asarray(ins["L0_attn_gamma"], dtype=np.float32).reshape(1, FD),
        "bet": np.asarray(ins["L0_attn_beta"], dtype=np.float32).reshape(1, FD),
        "ebias": np.asarray(ins["L0_ffn_e_bias"], dtype=np.float32).reshape(1, NE),
    }


def kernel(**inputs):
    nc = build_program(debug=0)
    in_maps = [make_inputs(inputs, c) for c in range(8)]
    res = run_bass_kernel_spmd(nc, in_maps, core_ids=list(range(8)))
    out = np.concatenate([np.asarray(r["y"]) for r in res.results], axis=0)
    return out.reshape(B, S, D)
